# version 05
# baseline (speedup 1.0000x reference)
import numpy as np
import ml_dtypes
import concourse.bass as bass
import concourse.mybir as mybir
from concourse.bass_utils import run_bass_kernel_spmd

F32 = mybir.dt.float32
BF16 = mybir.dt.bfloat16
AF = mybir.ActivationFunctionType
ALU = mybir.AluOpType
AX = mybir.AxisListType

D = 2048
H = 16
EH = 128
CH = 64
SEQ = 2048
BPC = 2
TA = 512
NAT = (BPC * SEQ) // TA
TPB_A = SEQ // TA
NTT = TA // 128
NCH = TA // CH
TB = 1024
NBT = (BPC * SEQ) // TB
HC = 44
FH = 5632
KC = 16
NCORE = 8

DEV_SCRATCH_OUT = False
DEV_TAPS = ()
import os as _os
DEV_NAT = int(_os.environ.get("KMK_NAT", NAT))
DEV_NBT = int(_os.environ.get("KMK_NBT", NBT))
DEV_PREP = int(_os.environ.get("KMK_PREP", 1))
DEV_WPASS = int(_os.environ.get("KMK_WPASS", 1))
DEV_STAGE = int(_os.environ.get("KMK_STAGE", 9))
DEV_LD = int(_os.environ.get("KMK_LD", 9))
DEV_LDOP = int(_os.environ.get("KMK_LDOP", 15))


class Sched:
    STREAMS = ("pe", "dv", "ac", "qs", "qg")
    ESEMS = ("pe", "dv", "ac", "qg")

    def __init__(self, nc):
        self.nc = nc
        self.ops = {s: [] for s in self.STREAMS}
        self.sem = {n: nc.alloc_semaphore(f"S_{n}") for n in self.ESEMS}
        self.cnt = {n: 0 for n in self.ESEMS}
        self.lastw = {s: {} for s in self.STREAMS}

    def _dmasem(self, ch):
        if ch not in self.sem:
            self.sem[ch] = self.nc.alloc_semaphore(f"S_{ch}")
            self.cnt[ch] = 0
        return self.sem[ch]

    def _flat(self, deps, out):
        for d in deps:
            if d is None:
                continue
            if isinstance(d, list):
                self._flat(d, out)
            else:
                out.append(d)

    def emit(self, stream, fn, deps=(), sig=None, dma_ch=None):
        flat = []
        self._flat(list(deps), flat)
        need = {}
        for (sname, val) in flat:
            if sname == stream and stream == "pe":
                continue
            if val > need.get(sname, 0):
                need[sname] = val
        waits = []
        for sname, val in need.items():
            cur = self.lastw[stream].get(sname, 0)
            if val > cur:
                waits.append((self.sem[sname], val))
                self.lastw[stream][sname] = val
        tok = None
        inc = None
        if dma_ch is not None:
            self._dmasem(dma_ch)
            self.cnt[dma_ch] += 16
            tok = (dma_ch, self.cnt[dma_ch])
            inc = (self.sem[dma_ch], 16)
        elif sig:
            self.cnt[sig] += 1
            tok = (sig, self.cnt[sig])
            inc = (self.sem[sig], 1)
        self.ops[stream].append((waits, fn, inc))
        return tok

    def pe(self, fn, deps=(), sig=False):
        return self.emit("pe", fn, deps, "pe" if sig else None)

    def dv(self, fn, deps=(), sig=False):
        return self.emit("dv", fn, deps, "dv" if sig else None)

    def ac(self, fn, deps=(), sig=False):
        return self.emit("ac", fn, deps, "ac" if sig else None)

    def qs(self, fn, deps=(), ch="g_qs"):
        return self.emit("qs", fn, deps, dma_ch=ch)

    def qg(self, fn, deps=(), ch="g_qg"):
        return self.emit("qg", fn, deps, dma_ch=ch)

    def gq(self, fn, deps=(), sig=False):
        return self.emit("qg", fn, deps, "qg" if sig else None)

    def qa(self, fn, deps=(), ch="g_qa"):
        return self.emit("ac", fn, deps, dma_ch=ch)

    def fence(self):
        return [(n, self.cnt[n]) for n in self.cnt]

    def replay(self, block):
        S = self

        def mk(stream):
            def run(eng):
                for waits, fn, inc in S.ops[stream]:
                    for (sem, val) in waits:
                        eng.wait_ge(sem, val)
                    ins = fn(eng)
                    if inc is not None:
                        ins.then_inc(*inc)
            return run

        block.tensor(mk("pe"))
        block.vector(mk("dv"))
        block.scalar(mk("ac"))
        block.sync(mk("qs"))
        block.gpsimd(mk("qg"))


class Bank:
    def __init__(self, ap):
        self.ap = ap
        self.ready = None
        self.free = []

    def wdeps(self):
        return list(self.free)

    def wrote(self, tok):
        if tok is not None:
            self.ready = tok
        self.free = []

    def rdeps(self):
        return [self.ready]

    def read(self, tok):
        if tok is not None:
            self.free.append(tok)


def _host_layouts(inp):
    wq = np.asarray(inp["L0_attn_Wq"], np.float32)
    wk = np.asarray(inp["L0_attn_Wk"], np.float32)
    wv = np.asarray(inp["L0_attn_Wv"], np.float32)
    wo = np.asarray(inp["L0_attn_Wo"], np.float32)
    wf = np.asarray(inp["L0_attn_Wf"], np.float32)
    nw = np.asarray(inp["L0_attn_norm_w"], np.float32)
    w1 = np.asarray(inp["L0_ffn_w1"], np.float32)
    w2 = np.asarray(inp["L0_ffn_w2"], np.float32)
    w3 = np.asarray(inp["L0_ffn_w3"], np.float32)

    def fc_major(w):
        return np.ascontiguousarray(
            w.reshape(KC, 128, KC, 128).transpose(2, 1, 0, 3).reshape(KC, 128, D))

    return {
        "wq_l": fc_major(wq),
        "wk_l": fc_major(wk),
        "wv_l": fc_major(wv),
        "wo_l": fc_major(wo),
        "wf_l": np.ascontiguousarray(
            wf.reshape(KC, 128, H).transpose(1, 0, 2).reshape(128, KC * H)),
        "nw_col": np.ascontiguousarray(nw.reshape(KC, 128).T),
        "w1_l": np.ascontiguousarray(
            w1.reshape(HC, 128, KC, 128).transpose(0, 3, 2, 1).reshape(HC, 128, D)),
        "w2_l": np.ascontiguousarray(
            w2.reshape(HC, 128, KC, 128).transpose(0, 3, 2, 1).reshape(HC, 128, D)),
        "w3_l": np.ascontiguousarray(
            w3.reshape(KC, 128, HC, 128).transpose(0, 3, 2, 1).reshape(KC, 128, FH)),
    }


def build_program():
    nc = bass.Bass("TRN2", target_bir_lowering=False, detect_race_conditions=False)

    x_hbm = nc.dram_tensor("x", [BPC, SEQ, D], F32, kind="ExternalInput")
    wq_l = nc.dram_tensor("wq_l", [KC, 128, D], F32, kind="ExternalInput")
    wk_l = nc.dram_tensor("wk_l", [KC, 128, D], F32, kind="ExternalInput")
    wv_l = nc.dram_tensor("wv_l", [KC, 128, D], F32, kind="ExternalInput")
    wo_l = nc.dram_tensor("wo_l", [KC, 128, D], F32, kind="ExternalInput")
    wf_l = nc.dram_tensor("wf_l", [128, KC * H], F32, kind="ExternalInput")
    nw_hbm = nc.dram_tensor("nw_col", [128, KC], F32, kind="ExternalInput")
    w1_l = nc.dram_tensor("w1_l", [HC, 128, D], F32, kind="ExternalInput")
    w2_l = nc.dram_tensor("w2_l", [HC, 128, D], F32, kind="ExternalInput")
    w3_l = nc.dram_tensor("w3_l", [KC, 128, FH], F32, kind="ExternalInput")
    out_hbm = nc.dram_tensor("out", [BPC, SEQ, D], F32, kind="ExternalOutput")

    skind = "ExternalOutput" if DEV_SCRATCH_OUT else "Internal"
    wq_b = nc.dram_tensor("wq_b", [KC, 128, D], BF16)
    wk_b = nc.dram_tensor("wk_b", [KC, 128, D], BF16)
    wv_b = nc.dram_tensor("wv_b", [KC, 128, D], BF16)
    wo_b = nc.dram_tensor("wo_b", [KC, 128, D], BF16)
    q1t = nc.dram_tensor("q1t", [HC, 128, D], BF16, kind=skind)
    q2t = nc.dram_tensor("q2t", [HC, 128, D], BF16, kind=skind)
    q3t = nc.dram_tensor("q3t", [KC, 128, FH], BF16, kind=skind)
    s_dram = nc.dram_tensor("s_dram", [D, BPC * SEQ], BF16, kind=skind)

    TAP_SPECS = {
        "xn0": ([128, H * TA], BF16), "q0": ([128, H * TA], BF16),
        "k0": ([128, H * TA], BF16), "v0": ([128, NTT * D], BF16),
        "ld0": ([16, TA], F32), "cum0": ([16, TA], F32),
        "o0": ([128, H * TA], BF16), "S0": ([128, H * EH], BF16),
        "hid0": ([128, HC * TB], BF16), "xnb0": ([128, KC * TB], BF16),
    }
    tap_hbm = {tn: nc.dram_tensor(f"tap_{tn}", list(TAP_SPECS[tn][0]),
                                  TAP_SPECS[tn][1], kind="ExternalOutput")
               for tn in DEV_TAPS}

    i128b_h = nc.inline_tensor(np.eye(128, dtype=ml_dtypes.bfloat16), "c_i128b")
    i128f_h = nc.inline_tensor(np.eye(128, dtype=np.float32), "c_i128f")
    ones_rf_h = nc.inline_tensor(np.ones((1, 128), np.float32), "c_ones_rf")
    ones_rb_h = nc.inline_tensor(np.ones((1, 128), ml_dtypes.bfloat16), "c_ones_rb")
    ones_cf_h = nc.inline_tensor(np.ones((128, 1), np.float32), "c_ones_cf")
    ones_cb_h = nc.inline_tensor(np.ones((128, 1), ml_dtypes.bfloat16), "c_ones_cb")
    m01 = np.tril(np.ones((CH, CH), np.float32)).T
    mask_np = np.tile(np.tile(m01, (1, H)), (2, 1)).astype(ml_dtypes.bfloat16)
    mask_h = nc.inline_tensor(mask_np, "c_mask")
    CVALS = [-0.5, 0.5, 1.0, 1e-6, 1e-5, 1e-8, 0.0]
    cval_h = nc.inline_tensor(
        np.repeat(np.array([CVALS], np.float32), 128, axis=0), "c_cvals")

    SA = nc.alloc_sbuf_tensor
    c_i128b = SA("s_i128b", [128, 128], BF16)
    c_i128f = SA("s_i128f", [128, 128], F32)
    c_ones_rf = SA("s_ones_rf", [1, 128], F32)
    c_ones_rb = SA("s_ones_rb", [1, 128], BF16)
    c_ones_cf = SA("s_ones_cf", [128, 1], F32)
    c_ones_cb = SA("s_ones_cb", [128, 1], BF16)
    c_mask = SA("s_mask", [128, H * CH], BF16)
    c_cval = SA("s_cvals", [128, len(CVALS)], F32)
    nw_col = SA("s_nw", [128, KC], F32)
    wf_sb = SA("s_wf", [128, KC * H], BF16)
    st_scale = SA("st_scale", [128, 32], F32)
    st_bias = SA("st_bias", [128, 32], F32)
    st_rstd2 = SA("st_rstd2", [128, 32], F32)
    p_acc = SA("p_acc", [128, 4], F32)
    p_tmp = SA("p_tmp", [128, 4], F32)
    p_inv = SA("p_inv", [128, 4], F32)
    p_sc = SA("p_sc", [1, 8], F32)
    a_st = SA("a_st", [128, 8], F32)

    for i, cv in enumerate(CVALS):
        nc.const_aps.register(F32, cv, c_cval[:, i:i + 1])

    base = nc.sbuf_base
    limit = nc.sbuf_top
    cur = [base]

    def place(name, shape, dtype):
        nbytes = int(np.prod(shape[1:])) * mybir.dt.size(dtype)
        off = (cur[0] + 31) & ~31
        cur[0] = off + nbytes
        assert cur[0] <= limit, f"SBUF overflow at {name}: {cur[0]} > {limit}"
        return nc.alloc_sbuf_tensor_at(name, list(shape), dtype, offset=off)

    a_x = [place(f"a_x{i}", [128, D], F32) for i in range(2)]
    a_xno = place("a_xno", [128, H * TA], BF16)
    a_q = place("a_q", [128, H * TA], BF16)
    a_k = place("a_k", [128, H * TA], BF16)
    a_v = place("a_v", [128, NTT * D], BF16)
    a_xt = [place(f"a_xt{i}", [128, D], BF16) for i in range(2)]
    a_vfs = [place(f"a_vfs{i}", [128, TA], BF16) for i in range(2)]
    a_vd = [place(f"a_vd{i}", [128, D], BF16) for i in range(2)]
    a_kt = [place(f"a_kt{i}", [128, D], BF16) for i in range(2)]
    a_wsl = [place(f"a_wsl{i}", [128, D], BF16) for i in range(2)]
    a_S = place("a_S", [128, H * EH], BF16)
    a_q1 = [place(f"a_q1_{i}", [128, H * CH], BF16) for i in range(2)]
    a_cq = [place(f"a_cq{i}", [128, H * CH], BF16) for i in range(2)]
    a_sc = place("a_sc", [128, H * CH], BF16)
    a_crow = [place(f"a_crow{i}", [1, H * 128], BF16) for i in range(2)]
    a_ld = place("a_ld", [16, TA], F32)
    a_cum = place("a_cum", [16, TA], F32)
    a_cfm = place("a_cfm", [16, TA], BF16)
    a_lsg = place("a_lsg", [16, 2 * TA], F32)
    a_lst = [place(f"a_lst{i}", [128, 2 * H], F32) for i in range(2)]
    a_ict = [place(f"a_ict{i}", [128, H], F32) for i in range(2)]
    a_dd = [place(f"a_dd{i}", [128, H], F32) for i in range(2)]
    a_tot = place("a_tot", [1, 4 * H], F32)
    a_eto = place("a_eto", [128, 2 * H], F32)
    a_sq = place("a_sq", [128, 1024], BF16)
    a_ssg = [place(f"a_ssg{i}", [128, TA], BF16) for i in range(2)]
    a_r1 = place("a_r1", [1, 2 * TA], F32)
    p_w = [place(f"p_w{i}", [128, D], F32) for i in range(3)]
    p_s1 = place("p_s1", [128, D], BF16)
    p_q = [place(f"p_q{i}", [128, D], BF16) for i in range(3)]
    wps_buf = [place(f"wps_buf{i}", [128, D], BF16) for i in range(2)]
    a_end = cur[0]

    cur[0] = base
    b_xn = place("b_xn", [128, KC * TB], BF16)
    b_hid = place("b_hid", [128, HC * TB], BF16)
    b_q1s = [place(f"b_q1s{i}", [128, D], BF16) for i in range(2)]
    b_q2s = [place(f"b_q2s{i}", [128, D], BF16) for i in range(2)]
    b_q3s = [place(f"b_q3s{i}", [128, FH], BF16) for i in range(2)]
    b_sig = [place(f"b_sig{i}", [128, TB], BF16) for i in range(2)]
    b_g = [place(f"b_g{i}", [128, TB], BF16) for i in range(2)]
    b_s = [place(f"b_s{i}", [128, TB], BF16) for i in range(2)]
    b_sg = [place(f"b_sg{i}", [128, TB], BF16) for i in range(2)]
    b_x = [place("b_x0", [128, D], F32)]
    b_xt = [place("b_xt0", [128, D], BF16)]
    b_xc = [place(f"b_xc{i}", [128, TB], F32) for i in range(2)]
    b_os = [place(f"b_os{i}", [128, TB], F32) for i in range(2)]
    b_end = cur[0]
    assert max(a_end, b_end) <= limit

    S = Sched(nc)
    PB = [Bank(nc.alloc_psum_tensor(f"pb{i}", [128, 512], F32)) for i in range(8)]

    def view(hd, p0, pn, off, dims):
        rl = hd.shape[-1]
        return bass.AP(hd, p0 * rl + off, [[rl, pn]] + dims)

    CONST = []
    for dst, src in [(c_i128b, i128b_h), (c_i128f, i128f_h), (c_ones_rf, ones_rf_h),
                     (c_ones_rb, ones_rb_h), (c_ones_cf, ones_cf_h),
                     (c_ones_cb, ones_cb_h), (c_mask, mask_h),
                     (c_cval, cval_h), (nw_col, nw_hbm)]:
        CONST.append(S.qs((lambda d, s_: lambda e: e.dma_start(out=d[:, :], in_=s_[:, :]))(dst, src), ch="const"))
    CONST.append(S.qg(lambda e: e.dma_start(out=wf_sb[:, :], in_=wf_l[:, :]), ch="constg"))

    wstore = {}
    WSL_STATE = {"free": [list(CONST), list(CONST)], "i": 0}

    def emit_wpass():
        wp_free = [list(CONST), list(CONST)]
        pi = 0
        for wi, (srcw, dstw) in enumerate(
                [(wq_l, wq_b), (wk_l, wk_b), (wv_l, wv_b), (wo_l, wo_b)] if DEV_WPASS else []):
            for fc in range(KC):
                bi = pi % 2
                pi += 1
                t_in = S.qg(
                    (lambda bb, s_, f_: lambda e: e.dma_start(out=bb[:, :], in_=s_[f_, :, :]))(
                        wps_buf[bi], srcw, fc),
                    deps=wp_free[bi], ch=f"wpl{bi}")
                t_out = S.qs(
                    (lambda bb, d_, f_: lambda e: e.dma_start(out=d_[f_, :, :], in_=bb[:, :]))(
                        wps_buf[bi], dstw, fc),
                    deps=[t_in], ch=f"wps{bi}")
                wp_free[bi] = [t_out]
                wstore[(wi, fc)] = t_out

    def _mk_units(srcs):
        units = []
        for mi, src, nch, width, dst in srcs:
            for ci in range(nch):
                if width == D:
                    units.append((mi, src, dst, ci, 0, D))
                else:
                    for (o, w_) in ((0, 2048), (2048, 2048), (4096, 1536)):
                        units.append((mi, src, dst, ci, o, w_))
        return units

    G_UNITS = _mk_units([(0, w1_l, HC, D, None), (1, w2_l, HC, D, None),
                         (2, w3_l, KC, FH, None)])
    Q_UNITS = _mk_units([(0, w1_l, HC, D, q1t), (1, w2_l, HC, D, q2t),
                         (2, w3_l, KC, FH, q3t)])
    PS = {"gi": 0, "qi": 0, "bfree": [list(CONST) for _ in range(3)],
          "qfree": [None] * 3, "accin": [False] * 3, "gdone": None, "inv": None,
          "s1free": None}

    def prep_gamma_step():
        if PS["gi"] >= len(G_UNITS):
            return
        mi, src, _, ci, off, w_ = G_UNITS[PS["gi"]]
        bi = PS["gi"] % 3
        PS["gi"] += 1
        buf = p_w[bi]
        t_in = S.qs(
            (lambda bb, s_, c_, o_, ww: lambda e: e.dma_start(
                out=bb[:, 0:ww], in_=s_[c_, :, o_:o_ + ww]))(buf, src, ci, off, w_),
            deps=PS["bfree"][bi], ch=f"pw{bi}")
        t_red = S.dv(
            (lambda bb, ww: lambda e: e.tensor_reduce(
                p_tmp[:, 3:4], bb[:, 0:ww], AX.X, ALU.add, apply_absolute_value=True))(buf, w_),
            deps=[t_in], sig=True)
        PS["bfree"][bi] = [t_red]
        if not PS["accin"][mi]:
            PS["accin"][mi] = True
            t_acc = S.dv(
                (lambda m_: lambda e: e.tensor_copy(p_acc[:, m_:m_ + 1], p_tmp[:, 3:4]))(mi),
                deps=[t_red], sig=True)
        else:
            t_acc = S.dv(
                (lambda m_: lambda e: e.tensor_add(
                    p_acc[:, m_:m_ + 1], p_acc[:, m_:m_ + 1], p_tmp[:, 3:4]))(mi),
                deps=[t_red], sig=True)
        PS["gdone"] = t_acc

    def prep_gamma_finalize():
        bank = PB[7]
        toks = []
        for mi in range(3):
            nelem = FH * D
            t_mm = S.pe(
                (lambda m_: lambda e: e.matmul(
                    bank.ap[0:1, 4 * m_:4 * m_ + 1], p_acc[:, m_:m_ + 1],
                    c_ones_cf[:, :], start=True, stop=True))(mi),
                deps=[PS["gdone"]] + bank.wdeps(), sig=True)
            bank.wrote(t_mm)
            t_c = S.dv(
                (lambda m_, n_: lambda e: e.tensor_scalar(
                    p_sc[0:1, m_:m_ + 1], bank.ap[0:1, 4 * m_:4 * m_ + 1],
                    1.0 / n_, 1e-8, ALU.mult, ALU.add))(mi, nelem),
                deps=bank.rdeps(), sig=True)
            bank.read(t_c)
            t_r = S.dv(
                (lambda m_: lambda e: e.reciprocal(
                    p_sc[0:1, 4 + m_:5 + m_], p_sc[0:1, m_:m_ + 1]))(mi),
                deps=[t_c], sig=True)
            t_b = S.pe(
                (lambda m_: lambda e: e.matmul(
                    bank.ap[:, 8 + m_:9 + m_], c_ones_rf[:, :],
                    p_sc[0:1, 4 + m_:5 + m_], start=True, stop=True))(mi),
                deps=[t_r] + bank.wdeps(), sig=True)
            bank.wrote(t_b)
            t_e = S.dv(
                (lambda m_: lambda e: e.tensor_copy(
                    p_inv[:, m_:m_ + 1], bank.ap[:, 8 + m_:9 + m_]))(mi),
                deps=bank.rdeps(), sig=True)
            bank.read(t_e)
            toks.append(t_e)
        PS["inv"] = toks

    def prep_quant_step():
        if PS["qi"] >= len(Q_UNITS):
            return
        mi, src, dst, ci, off, w_ = Q_UNITS[PS["qi"]]
        bi = PS["qi"] % 3
        PS["qi"] += 1
        buf = p_w[bi]
        qb = p_q[bi]
        t_in = S.qs(
            (lambda bb, s_, c_, o_, ww: lambda e: e.dma_start(
                out=bb[:, 0:ww], in_=s_[c_, :, o_:o_ + ww]))(buf, src, ci, off, w_),
            deps=PS["bfree"][bi] + PS["inv"], ch=f"pw{bi}")
        invc = p_inv[:, mi:mi + 1]
        t_s1 = S.ac(
            (lambda bb, ww, iv: lambda e: e.activation(
                p_s1[:, 0:ww], bb[:, 0:ww], AF.Sign, bias=-0.5, scale=iv))(buf, w_, invc),
            deps=[t_in, PS["s1free"]], sig=True)
        t_s2 = S.ac(
            (lambda bb, ww, iv, q_: lambda e: e.activation(
                q_[:, 0:ww], bb[:, 0:ww], AF.Sign, bias=0.5, scale=iv))(buf, w_, invc, qb),
            deps=[t_in, PS["qfree"][bi]], sig=True)
        PS["bfree"][bi] = [t_s2]
        t_add = S.gq(
            (lambda ww, q_: lambda e: e.tensor_add(
                q_[:, 0:ww], q_[:, 0:ww], p_s1[:, 0:ww]))(w_, qb),
            deps=[t_s1, t_s2], sig=True)
        PS["s1free"] = t_add
        t_out = S.qs(
            (lambda q_, d_, c_, o_, ww: lambda e: e.dma_start(
                out=d_[c_, :, o_:o_ + ww], in_=q_[:, 0:ww]))(qb, dst, ci, off, w_),
            deps=[t_add], ch=f"pq{bi}")
        PS["qfree"][bi] = t_out

    def prep_steps(n, quant):
        if not DEV_PREP:
            return
        for _ in range(n):
            (prep_quant_step if quant else prep_gamma_step)()

    def mm(out_ap, l_ap, r_ap, start, stop, deps=(), sig=False):
        return S.pe(
            (lambda o_, l_, r_, s0, s1: lambda e: e.matmul(o_, l_, r_, start=s0, stop=s1))(
                out_ap, l_ap, r_ap, start, stop),
            deps, sig)

    def load_wslab(wi, fc, at, deps_extra=()):
        bi = WSL_STATE["i"] % 2
        WSL_STATE["i"] += 1
        srcs_f32 = [wq_l, wk_l, wv_l, wo_l]
        srcs_b16 = [wq_b, wk_b, wv_b, wo_b]
        deps = WSL_STATE["free"][bi] + list(deps_extra)
        if at == 0 or not DEV_WPASS:
            tok = S.qg(
                (lambda bb, s_, f_: lambda e: e.dma_start(out=bb[:, :], in_=s_[f_, :, :]))(
                    a_wsl[bi], srcs_f32[wi], fc), deps=deps, ch=f"wslg{bi}")
        else:
            tok = S.qs(
                (lambda bb, s_, f_: lambda e: e.dma_start(out=bb[:, :], in_=s_[f_, :, :]))(
                    a_wsl[bi], srcs_b16[wi], fc),
                deps=deps + [wstore[(wi, fc)]], ch=f"wsl{bi}")
        return bi, tok

    def wsl_release(bi, toks):
        WSL_STATE["free"][bi] = list(toks)

    x_free = [[], []]
    xt_free = [[], []]
    vfs_free = [[], []]
    vd_free = [[], []]
    kt_free = [[], []]
    q1_free = [[], []]
    cq_free = [[], []]
    crow_free = [[], []]
    lst_free = [[], []]
    ict_free = [[], []]
    eto_free = [[], []]
    dd_free = [[], []]
    sc_free = [[], []]
    ssg_free = [[], []]
    sq_free = [[], []]
    S_write = [None] * H
    ld_free = []
    xno_writable = []
    xno_free = []
    q_free, k_free, v_free = [], [], []

    for at in range(DEV_NAT):
        b = at // TPB_A
        t0 = (at % TPB_A) * TA
        g0 = at * TA
        first_of_seq = (at % TPB_A) == 0

        xn_evac_toks = []
        for tt in range(NTT):
            bi = tt % 2
            ttg = g0 // 128 + tt
            t_x = S.qs(
                (lambda b_, bb, r0: lambda e: e.dma_start(
                    out=a_x[b_][:, :], in_=x_hbm[bb, r0:r0 + 128, :]))(bi, b, t0 + tt * 128),
                deps=x_free[bi] + CONST, ch=f"x{bi}")
            t_sq = S.ac(
                (lambda b_, xb: lambda e: e.activation(
                    xb[:, :], a_x[b_][:, :], AF.Square, accum_out=a_st[:, 1:2]))(bi, a_xt[bi]),
                deps=[t_x] + xt_free[bi], sig=True)
            t_sm = S.dv(
                (lambda b_, xb: lambda e: e.tensor_scalar(
                    xb[:, :], a_x[b_][:, :], 1.0, 0.0, ALU.mult, ALU.add,
                    accum_out=a_st[:, 0:1]))(bi, a_xt[bi]),
                deps=[t_sq], sig=True)
            t_mu = S.dv(
                lambda e: e.tensor_scalar(a_st[:, 3:4], a_st[:, 0:1], 1.0 / D, None, ALU.mult),
                deps=[t_sm], sig=True)
            t_e2 = S.dv(
                lambda e: e.tensor_scalar(a_st[:, 1:2], a_st[:, 1:2], 1.0 / D, None, ALU.mult),
                deps=[t_sq], sig=True)
            t_nv = S.dv(
                lambda e: e.scalar_tensor_tensor(
                    a_st[:, 2:3], a_st[:, 3:4], a_st[:, 3:4], a_st[:, 1:2],
                    ALU.mult, ALU.subtract),
                deps=[t_mu, t_e2], sig=True)
            t_ln = S.ac(
                lambda e: e.activation(a_st[:, 4:5], a_st[:, 2:3], AF.Ln,
                                       scale=-1.0, bias=1e-5),
                deps=[t_nv], sig=True)
            t_rs = S.ac(
                (lambda c: lambda e: e.activation(
                    st_scale[:, c:c + 1], a_st[:, 4:5], AF.Exp, scale=-0.5))(ttg),
                deps=[t_ln], sig=True)
            t_nb = S.dv(
                (lambda c: lambda e: e.tensor_scalar(
                    st_bias[:, c:c + 1], st_scale[:, c:c + 1], a_st[:, 3:4], -1.0,
                    ALU.mult, ALU.mult))(ttg),
                deps=[t_rs], sig=True)
            t_ap = S.ac(
                (lambda b_, xb, c: lambda e: e.activation(
                    xb[:, :], a_x[b_][:, :], AF.Identity,
                    scale=st_scale[:, c:c + 1], bias=st_bias[:, c:c + 1]))(bi, a_xt[bi], ttg),
                deps=[t_nb, t_sm], sig=True)
            x_free[bi] = [t_ap, t_sm]
            grp_ev = []
            for g in range(4):
                bk = PB[4 + g % 2]
                t_last = None
                for j in range(4):
                    fcb = g * 4 + j
                    t_last = mm(
                        bk.ap[:, j * 128:(j + 1) * 128],
                        a_xt[bi][:, fcb * 128:(fcb + 1) * 128],
                        c_i128b[:, :], True, True,
                        deps=[t_ap] + (bk.wdeps() if j == 0 else []),
                        sig=(j == 3))
                bk.wrote(t_last)
                t_ev = S.ac(
                    (lambda bkap, g_, tt_: lambda e: e.copy(
                        view(a_xno, 0, 128, (g_ * 4) * TA + tt_ * 128, [[TA, 4], [1, 128]]),
                        bkap[:, :].rearrange("p (a b) -> p a b", a=4)))(bk.ap, g, tt),
                    deps=bk.rdeps() + xno_free, sig=True)
                bk.read(t_ev)
                grp_ev.append(t_ev)
            xt_free[bi] = grp_ev
            xn_evac_toks.extend(grp_ev)

        if DEV_STAGE < 2:
            continue
        bk6 = PB[6]
        t_last = None
        for kc in range(KC):
            t_last = mm(bk6.ap[0:16, 0:512], wf_sb[:, kc * H:(kc + 1) * H],
                        a_xno[:, kc * TA:(kc + 1) * TA], kc == 0, kc == KC - 1,
                        deps=xn_evac_toks + (bk6.wdeps() if kc == 0 else []),
                        sig=(kc == KC - 1))
        bk6.wrote(t_last)
        t_abs = S.ac(lambda e: e.activation(a_lsg[:, 0:TA], bk6.ap[0:16, 0:512], AF.Abs),
                     deps=bk6.rdeps() + ld_free, sig=True)
        if DEV_LD < 2:
            bk6.read(t_abs)
            continue
        t_exp = t_abs
        t_ln1 = t_abs
        if DEV_LDOP & 1:
            t_exp = S.ac(lambda e: e.activation(a_lsg[:, TA:2 * TA], a_lsg[:, 0:TA],
                                                AF.Exp, scale=-1.0), deps=[t_abs], sig=True)
        if DEV_LDOP & 2:
            t_ln1 = S.ac(lambda e: e.activation(a_lsg[:, 0:TA], a_lsg[:, TA:2 * TA],
                                                AF.Ln, bias=c_cval[0:16, 2:3]),
                         deps=[t_exp] + CONST, sig=True)
        t_min = t_abs
        if DEV_LDOP & 4:
            t_min = S.dv(lambda e: e.tensor_scalar(a_ld[:, :], bk6.ap[0:16, 0:512],
                                                   0.0, None, ALU.min),
                         deps=bk6.rdeps() + ld_free + [t_abs], sig=True)
            bk6.read(t_min)
        bk6.read(t_abs)
        t_ld = t_min
        if DEV_LDOP & 8:
            t_ld = S.dv(lambda e: e.tensor_sub(a_ld[:, :], a_ld[:, :], a_lsg[:, 0:TA]),
                        deps=[t_ln1, t_min], sig=True)
        if DEV_LD < 3:
            continue
        t_cums = []
        for c in range(NCH):
            t_cums.append(S.dv(
                (lambda c_: lambda e: e.tensor_tensor_scan(
                    a_cum[:, c_ * CH:(c_ + 1) * CH], a_ld[:, c_ * CH:(c_ + 1) * CH],
                    a_ld[:, c_ * CH:(c_ + 1) * CH], 0.0, ALU.add, ALU.bypass))(c),
                deps=[t_ld] + ld_free, sig=True))
        t_cfm = S.ac(lambda e: e.activation(a_cfm[:, :], a_cum[:, :], AF.Exp),
                     deps=t_cums + ld_free, sig=True)
        if at == 0 and "ld0" in tap_hbm:
            S.qa((lambda: lambda e: e.dma_start(out=tap_hbm["ld0"][:, :], in_=a_ld[:, :]))(),
                 deps=[t_ld])
        if at == 0 and "cum0" in tap_hbm:
            S.qa((lambda: lambda e: e.dma_start(out=tap_hbm["cum0"][:, :], in_=a_cum[:, :]))(),
                 deps=t_cums)

        if DEV_STAGE < 3:
            continue
        def project(wi, dst, act, dst_free, v_mode=False):
            evs = []
            for fc in range(KC):
                sbi, t_w = load_wslab(wi, fc, at)
                bk = PB[fc % 4]
                t_last = None
                for kc in range(KC):
                    t_last = mm(bk.ap[:, 0:512],
                                a_wsl[sbi][:, kc * 128:(kc + 1) * 128],
                                a_xno[:, kc * TA:(kc + 1) * TA],
                                kc == 0, kc == KC - 1,
                                deps=([t_w] + xn_evac_toks +
                                      (bk.wdeps() if kc == 0 else [])),
                                sig=(kc == KC - 1))
                bk.wrote(t_last)
                wsl_release(sbi, [t_last])
                if not v_mode:
                    t_ev = S.ac(
                        (lambda bkap, f_: lambda e: e.activation(
                            dst[:, f_ * TA:(f_ + 1) * TA], bkap[:, 0:512], act))(bk.ap, fc),
                        deps=bk.rdeps() + dst_free, sig=True)
                    bk.read(t_ev)
                    evs.append(t_ev)
                else:
                    vb = fc % 2
                    t_ev = S.ac(
                        (lambda bkap, v_: lambda e: e.copy(a_vfs[v_][:, :], bkap[:, 0:512]))(
                            bk.ap, vb),
                        deps=bk.rdeps() + vfs_free[vb], sig=True)
                    bk.read(t_ev)
                    tbk = PB[4 + fc % 2]
                    t_tl = None
                    for tb in range(4):
                        t_tl = mm(tbk.ap[:, tb * 128:(tb + 1) * 128],
                                  a_vfs[vb][:, tb * 128:(tb + 1) * 128],
                                  c_i128b[:, :], True, True,
                                  deps=[t_ev] + (tbk.wdeps() if tb == 0 else []),
                                  sig=(tb == 3))
                    tbk.wrote(t_tl)
                    t_vv = S.dv(
                        (lambda bkap, f_: lambda e: e.tensor_copy(
                            view(a_v, 0, 128, f_ * 128, [[D, 4], [1, 128]]),
                            bkap[:, :].rearrange("p (a b) -> p a b", a=4)))(tbk.ap, fc),
                        deps=tbk.rdeps() + dst_free, sig=True)
                    tbk.read(t_vv)
                    vfs_free[vb] = [t_vv]
                    evs.append(t_vv)
            return evs

        q_evs = project(0, a_q, AF.Silu, q_free)
        k_evs = project(1, a_k, AF.Silu, k_free)
        v_evs = project(2, None, None, v_free, v_mode=True)

        if at == 0 and "xn0" in tap_hbm:
            S.qa(lambda e: e.dma_start(out=tap_hbm["xn0"][:, :], in_=a_xno[:, :]),
                 deps=xn_evac_toks, ch='tap')
        if at == 0 and "q0" in tap_hbm:
            S.qa(lambda e: e.dma_start(out=tap_hbm["q0"][:, :], in_=a_q[:, :]), deps=q_evs)
        if at == 0 and "k0" in tap_hbm:
            S.qa(lambda e: e.dma_start(out=tap_hbm["k0"][:, :], in_=a_k[:, :]), deps=k_evs)
        if at == 0 and "v0" in tap_hbm:
            S.qa(lambda e: e.dma_start(out=tap_hbm["v0"][:, :], in_=a_v[:, :]), deps=v_evs)

        new_q_free, new_k_free, new_v_free = [], [], []
        new_ld_free = []
        o_ev_all = []
        if DEV_STAGE < 4:
            continue

        for tt in range(NTT):
            tp = tt % 2
            bk7 = PB[7]
            t_t1 = mm(bk7.ap[:, 0:16], a_ld[:, tt * 128:(tt + 1) * 128],
                      c_i128f[0:16, 0:16], True, True,
                      deps=[t_ld] + bk7.wdeps(), sig=False)
            t_t2 = mm(bk7.ap[:, 16:32], a_cum[:, tt * 128:(tt + 1) * 128],
                      c_i128f[0:16, 0:16], True, True,
                      deps=t_cums, sig=True)
            bk7.wrote(t_t2)
            new_ld_free.append(t_t2)
            t_le = S.ac(
                (lambda bkap, t_: lambda e: e.copy(a_lst[t_][:, :], bkap[:, 0:32]))(
                    bk7.ap, tp),
                deps=bk7.rdeps() + lst_free[tp], sig=True)
            bk7.read(t_le)
            t_ic = S.ac(
                (lambda t_: lambda e: e.activation(
                    a_ict[t_][:, :], a_lst[t_][:, 16:32], AF.Exp, scale=-1.0))(tp),
                deps=[t_le] + ict_free[tp], sig=True)
            ict_free[tp] = []
            lst_free[tp] = []
            t_crows = []
            for h in range(H):
                t_crows.append(S.qa(
                    (lambda t_, h_, tt_: lambda e: e.dma_start(
                        out=a_crow[t_][0:1, h_ * 128:(h_ + 1) * 128],
                        in_=a_cfm[h_:h_ + 1, tt_ * 128:(tt_ + 1) * 128]))(tp, h, tt),
                    deps=[t_cfm] + crow_free[tp], ch=f"crow{tp}"))
            crow_free[tp] = []
            new_ld_free.extend(t_crows[-1:])
            kt_evs = []
            for g in range(4):
                bk = PB[2 + g % 2]
                t_tl = None
                for j in range(4):
                    h = g * 4 + j
                    t_tl = mm(bk.ap[:, j * 128:(j + 1) * 128],
                              a_k[:, h * TA + tt * 128: h * TA + (tt + 1) * 128],
                              c_i128b[:, :], True, True,
                              deps=[k_evs[h]] + (bk.wdeps() if j == 0 else []),
                              sig=(j == 3))
                bk.wrote(t_tl)
                t_ke = S.dv(
                    (lambda bkap, t_, g_: lambda e: e.tensor_copy(
                        a_kt[t_][:, g_ * 512:(g_ + 1) * 512], bkap[:, :]))(bk.ap, tp, g),
                    deps=bk.rdeps() + kt_free[tp], sig=True)
                bk.read(t_ke)
                kt_evs.append(t_ke)
            kt_free[tp] = []

            if DEV_STAGE < 5:
                continue
            for half in range(2):
                c = 2 * tt + half
                p0 = 64 * half
                pr = c % 2
                bk6 = PB[6]
                t_tr = mm(bk6.ap[0:1, 0:16], c_ones_cf[p0:p0 + 64, :],
                          a_lst[tp][p0:p0 + 64, 0:16], True, True,
                          deps=[t_le] + bk6.wdeps(), sig=True)
                bk6.wrote(t_tr)
                t_te = S.ac(
                    (lambda bkap, pr_: lambda e: e.copy(
                        a_tot[0:1, pr_ * 16:(pr_ + 1) * 16], bkap[0:1, 0:16]))(bk6.ap, pr),
                    deps=bk6.rdeps(), sig=True)
                bk6.read(t_te)
                t_ee = S.ac(
                    (lambda pr_: lambda e: e.activation(
                        a_tot[0:1, 32 + pr_ * 16:32 + (pr_ + 1) * 16],
                        a_tot[0:1, pr_ * 16:(pr_ + 1) * 16], AF.Exp))(pr),
                    deps=[t_te], sig=True)
                t_tb = mm(bk6.ap[:, 16:32], c_ones_rf[:, :],
                          a_tot[0:1, pr * 16:(pr + 1) * 16], True, True,
                          deps=[t_te] + bk6.wdeps(), sig=True)
                bk6.wrote(t_tb)
                t_dd = S.dv(
                    (lambda bkap, t_, q0_: lambda e: e.tensor_sub(
                        a_dd[t_][q0_:q0_ + 64, :], bkap[q0_:q0_ + 64, 16:32],
                        a_lst[t_][q0_:q0_ + 64, 16:32]))(bk6.ap, tp, p0),
                    deps=bk6.rdeps() + [t_le] + dd_free[tp], sig=True)
                bk6.read(t_dd)
                t_de = S.ac(
                    (lambda t_, q0_: lambda e: e.activation(
                        a_dd[t_][q0_:q0_ + 64, :], a_dd[t_][q0_:q0_ + 64, :], AF.Exp))(tp, p0),
                    deps=[t_dd], sig=True)
                t_eb = mm(bk6.ap[:, 32:48], c_ones_rf[:, :],
                          a_tot[0:1, 32 + pr * 16:32 + (pr + 1) * 16], True, True,
                          deps=[t_ee] + bk6.wdeps(), sig=True)
                bk6.wrote(t_eb)
                t_ev = S.ac(
                    (lambda bkap, pr_: lambda e: e.copy(
                        a_eto[:, pr_ * 16:(pr_ + 1) * 16], bkap[:, 32:48]))(bk6.ap, pr),
                    deps=bk6.rdeps() + eto_free[pr], sig=True)
                bk6.read(t_ev)
                eto_free[pr] = []
                t_vd = S.dv(
                    (lambda t_, q0_, tt_: lambda e: e.tensor_mul(
                        view(a_vd[t_], q0_, 64, 0, [[128, H], [1, 128]]),
                        view(a_v, q0_, 64, tt_ * D, [[128, H], [1, 128]]),
                        view(a_dd[t_], q0_, 64, 0, [[1, H], [0, 128]])))(tp, p0, tt),
                    deps=[t_de] + v_evs + vd_free[tp], sig=True)
                vd_free[tp] = []
                cq_toks = []
                for rnd in range(2):
                    t_cb = mm(bk7.ap[:, 0:512], c_ones_rb[:, :],
                              bass.AP(a_crow[tp], rnd * 1024 + half * 64,
                                      [[2048, 1], [128, 8], [1, 64]]),
                              True, True,
                              deps=t_crows + bk7.wdeps(), sig=True)
                    bk7.wrote(t_cb)
                    t_ce = S.ac(
                        (lambda bkap, pr_, r_: lambda e: e.copy(
                            a_cq[pr_][:, r_ * 512:(r_ + 1) * 512], bkap[:, 0:512]))(
                            bk7.ap, pr, rnd),
                        deps=bk7.rdeps() + cq_free[pr], sig=True)
                    bk7.read(t_ce)
                    cq_toks.append(t_ce)
                cq_free[pr] = []
                t_q1 = S.dv(
                    (lambda pr_, c_: lambda e: e.tensor_mul(
                        view(a_q1[pr_], 0, 128, 0, [[CH, H], [1, CH]]),
                        view(a_q, 0, 128, c_ * CH, [[TA, H], [1, CH]]),
                        view(a_cq[pr_], 0, 128, 0, [[CH, H], [1, CH]])))(pr, c),
                    deps=cq_toks + q_evs + q1_free[pr], sig=True)
                q1_free[pr] = []
                for g in range(2):
                    bk = PB[4 + g]
                    for j in range(8):
                        h = g * 8 + j
                        mm(bk.ap[p0:p0 + 64, j * 64:(j + 1) * 64],
                           a_k[:, h * TA + c * CH: h * TA + (c + 1) * CH],
                           a_q1[pr][:, h * CH:(h + 1) * CH],
                           True, True,
                           deps=([t_q1, k_evs[h]] + (bk.wdeps() if j == 0 else [])),
                           sig=(j == 7))
                    tok = ("pe", S.cnt["pe"])
                    bk.wrote(tok)
                sc_toks = []
                for g in range(2):
                    bk = PB[4 + g]
                    t_s1 = S.dv(
                        (lambda bkap, q0_, g_, t_: lambda e: e.tensor_mul(
                            view(a_sc, q0_, 64, g_ * 512, [[CH, 8], [1, CH]]),
                            bkap[q0_:q0_ + 64, :].rearrange("p (h i) -> p h i", h=8),
                            view(a_ict[t_], q0_, 64, g_ * 8, [[1, 8], [0, CH]])))(
                            bk.ap, p0, g, tp),
                        deps=bk.rdeps() + [t_ic] + sc_free[pr], sig=True)
                    bk.read(t_s1)
                    t_s2 = S.dv(
                        (lambda q0_, g_: lambda e: e.tensor_mul(
                            a_sc[q0_:q0_ + 64, g_ * 512:(g_ + 1) * 512],
                            a_sc[q0_:q0_ + 64, g_ * 512:(g_ + 1) * 512],
                            c_mask[q0_:q0_ + 64, g_ * 512:(g_ + 1) * 512]))(p0, g),
                        deps=[t_s1], sig=True)
                    sc_toks.append(t_s2)
                    ict_free[tp].append(t_s1)
                sc_free[pr] = []
                skip_inter = (first_of_seq and c == 0)
                for h in range(H):
                    go = h // 8
                    bko = PB[go]
                    first_in_bank = (h % 8 == 0)
                    deps_o = ([sc_toks[go]] + v_evs +
                              (bko.wdeps() if first_in_bank else []))
                    mm(bko.ap[:, (h % 8) * 64:(h % 8 + 1) * 64],
                       view(a_v, p0, 64, tt * D + h * 128, [[1, 128]]),
                       a_sc[p0:p0 + 64, h * 64:(h + 1) * 64],
                       True, skip_inter, deps=deps_o,
                       sig=(skip_inter and h % 8 == 7))
                    if not skip_inter:
                        mm(bko.ap[:, (h % 8) * 64:(h % 8 + 1) * 64],
                           a_S[:, h * 128:(h + 1) * 128],
                           a_q1[pr][:, h * CH:(h + 1) * CH],
                           False, True, deps=[S_write[h], t_q1],
                           sig=(h % 8 == 7))
                    if h % 8 == 7:
                        PB[go].wrote(("pe", S.cnt["pe"]))
                o_evs = []
                for g in range(2):
                    bk = PB[g]
                    t_oe = S.ac(
                        (lambda bkap, g_, c_: lambda e: e.copy(
                            view(a_xno, 0, 128, (g_ * 8) * TA + c_ * CH, [[TA, 8], [1, CH]]),
                            bkap[:, :].rearrange("p (h i) -> p h i", h=8)))(bk.ap, g, c),
                        deps=bk.rdeps(), sig=True)
                    bk.read(t_oe)
                    o_evs.append(t_oe)
                o_ev_all.extend(o_evs)
                for g4 in range(4):
                    bku = PB[2 + g4 % 2]
                    t_uu = None
                    for j in range(4):
                        h = g4 * 4 + j
                        t_uu = mm(bku.ap[:, j * 128:(j + 1) * 128],
                                  view(a_kt[tp], p0, 64, h * 128, [[1, 128]]),
                                  view(a_vd[tp], p0, 64, h * 128, [[1, 128]]),
                                  True, True,
                                  deps=([kt_evs[h // 4], t_vd] +
                                        (bku.wdeps() if j == 0 else [])),
                                  sig=(j == 3))
                    bku.wrote(t_uu)
                    for j in range(4):
                        h = g4 * 4 + j
                        if skip_inter:
                            t_su = S.dv(
                                (lambda bkap, h_, j_: lambda e: e.tensor_copy(
                                    a_S[:, h_ * 128:(h_ + 1) * 128],
                                    bkap[:, j_ * 128:(j_ + 1) * 128]))(bku.ap, h, j),
                                deps=bku.rdeps() + o_evs, sig=True)
                        else:
                            t_su = S.dv(
                                (lambda bkap, h_, j_, pr_: lambda e: e.scalar_tensor_tensor(
                                    a_S[:, h_ * 128:(h_ + 1) * 128],
                                    a_S[:, h_ * 128:(h_ + 1) * 128],
                                    a_eto[:, pr_ * 16 + h_:pr_ * 16 + h_ + 1],
                                    bkap[:, j_ * 128:(j_ + 1) * 128],
                                    ALU.mult, ALU.add))(bku.ap, h, j, pr),
                                deps=bku.rdeps() + [t_ev] + o_evs, sig=True)
                        S_write[h] = t_su
                        bku.read(t_su)
                        eto_free[pr].append(t_su)
                q1_free[pr] = [("pe", S.cnt["pe"])]
                cq_free[pr] = [t_q1]
                sc_free[pr] = [("pe", S.cnt["pe"])]
                dd_free[tp] = [t_vd]
                vd_free[tp] = [S_write[H - 1]]
                kt_free[tp] = [S_write[H - 1]]
                crow_free[tp] = [("pe", S.cnt["pe"])]
                lst_free[tp] = [t_dd, t_tr]
            prep_steps(9, quant=(at >= NAT // 2))

        if at == NAT // 2 - 1:
            while PS["gi"] < len(G_UNITS):
                prep_gamma_step()
            prep_gamma_finalize()

        if at == 0 and "S0" in tap_hbm:
            S.qa(lambda e: e.dma_start(out=tap_hbm["S0"][:, :], in_=a_S[:, :]),
                 deps=list(S_write, ch='tap'))
        if at == 0 and "o0" in tap_hbm:
            S.qa(lambda e: e.dma_start(out=tap_hbm["o0"][:, :], in_=a_xno[:, :]),
                 deps=o_ev_all, ch='tap')

        if DEV_STAGE < 6:
            continue
        bk6 = PB[6]
        nw_toks = []
        t_ssq_last = None
        for fc in range(KC):
            sqh = fc % 2
            t_sq = S.ac(
                (lambda f_, s_: lambda e: e.activation(
                    a_sq[:, s_ * 512:(s_ + 1) * 512],
                    a_xno[:, f_ * TA:(f_ + 1) * TA], AF.Square))(fc, sqh),
                deps=o_ev_all + sq_free[sqh], sig=True)
            t_mmq = mm(bk6.ap[0:1, 0:512], c_ones_cb[:, :],
                       a_sq[:, sqh * 512:(sqh + 1) * 512],
                       fc == 0, fc == KC - 1,
                       deps=[t_sq] + (bk6.wdeps() if fc == 0 else []),
                       sig=True)
            sq_free[sqh] = [t_mmq]
            t_nw = S.dv(
                (lambda f_: lambda e: e.tensor_scalar(
                    a_xno[:, f_ * TA:(f_ + 1) * TA],
                    a_xno[:, f_ * TA:(f_ + 1) * TA],
                    nw_col[:, f_:f_ + 1], None, ALU.mult))(fc),
                deps=[t_sq], sig=True)
            nw_toks.append(t_nw)
            t_ssq_last = t_mmq
        bk6.wrote(t_ssq_last)
        t_lnr = S.ac(lambda e: e.activation(a_r1[0:1, 0:TA], bk6.ap[0:1, 0:512],
                                            AF.Ln, scale=1.0 / D, bias=c_cval[0:1, 3:4]),
                     deps=bk6.rdeps(), sig=True)
        bk6.read(t_lnr)
        t_rr = S.ac(lambda e: e.activation(a_r1[0:1, TA:2 * TA], a_r1[0:1, 0:TA],
                                           AF.Exp, scale=-0.5), deps=[t_lnr], sig=True)
        bk7 = PB[7]
        t_rc = None
        for k4 in range(4):
            t_rc = mm(bk7.ap[:, k4:k4 + 1],
                      a_r1[0:1, TA + k4 * 128:TA + (k4 + 1) * 128],
                      c_ones_rf[0:1, 0:1], True, True,
                      deps=[t_rr] + (bk7.wdeps() if k4 == 0 else []), sig=(k4 == 3))
        bk7.wrote(t_rc)
        t_rce = S.dv(
            (lambda at_: lambda e: e.tensor_copy(
                st_rstd2[:, at_ * 4:(at_ + 1) * 4], bk7.ap[:, 0:4]))(at),
            deps=bk7.rdeps(), sig=True)
        bk7.read(t_rce)

        s_mm_toks = []
        for fc in range(KC):
            sbi, t_w = load_wslab(3, fc, at)
            bk = PB[fc % 4]
            t_last = None
            for kc in range(KC):
                t_last = mm(bk.ap[:, 0:512],
                            a_wsl[sbi][:, kc * 128:(kc + 1) * 128],
                            a_xno[:, kc * TA:(kc + 1) * TA],
                            kc == 0, kc == KC - 1,
                            deps=([t_w] + nw_toks + (bk.wdeps() if kc == 0 else [])),
                            sig=(kc == KC - 1))
            bk.wrote(t_last)
            wsl_release(sbi, [t_last])
            s_mm_toks.append(t_last)
            sgi = fc % 2
            t_se = S.ac(
                (lambda bkap, s_: lambda e: e.copy(a_ssg[s_][:, :], bkap[:, 0:512]))(
                    bk.ap, sgi),
                deps=bk.rdeps() + ssg_free[sgi], sig=True)
            bk.read(t_se)
            t_sd = S.qa(
                (lambda s_, f_, gg: lambda e: e.dma_start(
                    out=s_dram[f_ * 128:(f_ + 1) * 128, gg:gg + TA], in_=a_ssg[s_][:, :]))(
                    sgi, fc, g0),
                deps=[t_se], ch=f"ssg{sgi}")
            ssg_free[sgi] = [t_sd]

        q_free = [t_q1]
        k_free = [("pe", S.cnt["pe"])]
        v_free = [("pe", S.cnt["pe"]), t_vd]
        ld_free = new_ld_free + [t_cfm]
        xno_free = s_mm_toks[-1:] + nw_toks[-1:]
        if at == 0 and DEV_WPASS:
            emit_wpass()

    if DEV_PREP:
        while PS["gi"] < len(G_UNITS):
            prep_gamma_step()
        if PS["inv"] is None:
            prep_gamma_finalize()
        while PS["qi"] < len(Q_UNITS):
            prep_quant_step()

    FB = S.fence()
    bxn_free = []
    bhid_free = []
    bq1_free = [FB, FB]
    bq2_free = [FB, FB]
    bq3_free = [FB, FB]
    bsig_free = [[], []]
    bg_free = [[], []]
    bs_free = [FB, FB]
    bsg_free = [[], []]
    bx_free = [FB]
    bxt_free = [FB]
    bxc_free = [FB, FB]
    bos_free = [[], []]
    OUT_TOKS = []

    for bt in range(DEV_NBT):
        b = bt // (NBT // BPC)
        t0 = (bt % (NBT // BPC)) * TB
        g0 = bt * TB
        gtt0 = g0 // 128

        xn_evs = []
        for tt in range(TB // 128):
            bi = 0
            ttg = gtt0 + tt
            t_x = S.qs(
                (lambda b_, bb, r0: lambda e: e.dma_start(
                    out=b_x[b_][:, :], in_=x_hbm[bb, r0:r0 + 128, :]))(bi, b, t0 + tt * 128),
                deps=[bx_free[bi]] + FB, ch="bx")
            t_ap = S.ac(
                (lambda b_, c: lambda e: e.activation(
                    b_xt[b_][:, :], b_x[b_][:, :], AF.Identity,
                    scale=st_scale[:, c:c + 1], bias=st_bias[:, c:c + 1]))(bi, ttg),
                deps=[t_x, bxt_free[bi]] + FB, sig=True)
            bx_free[bi] = t_ap
            grp = []
            for g in range(4):
                bk = PB[6 + g % 2]
                t_tl = None
                for j in range(4):
                    fcb = g * 4 + j
                    t_tl = mm(bk.ap[:, j * 128:(j + 1) * 128],
                              b_xt[bi][:, fcb * 128:(fcb + 1) * 128],
                              c_i128b[:, :], True, True,
                              deps=[t_ap] + (bk.wdeps() if j == 0 else []) + FB,
                              sig=(j == 3))
                bk.wrote(t_tl)
                t_ev = S.ac(
                    (lambda bkap, g_, tt_: lambda e: e.copy(
                        view(b_xn, 0, 128, (g_ * 4) * TB + tt_ * 128, [[TB, 4], [1, 128]]),
                        bkap[:, :].rearrange("p (a b) -> p a b", a=4)))(bk.ap, g, tt),
                    deps=bk.rdeps() + bxn_free, sig=True)
                bk.read(t_ev)
                grp.append(t_ev)
            bxt_free[bi] = grp
            xn_evs.extend(grp)
        bxn_free = []

        if bt == 0 and "xnb0" in tap_hbm:
            S.qa(lambda e: e.dma_start(out=tap_hbm["xnb0"][:, :], in_=b_xn[:, :]),
                 deps=xn_evs, ch='tap')

        hid_toks = []
        for hc in range(HC):
            sbi = hc % 2
            t_w1 = S.qs(
                (lambda bb, h_: lambda e: e.dma_start(out=bb[:, :], in_=q1t[h_, :, :]))(
                    b_q1s[sbi], hc),
                deps=[bq1_free[sbi]] + FB, ch=f"bq1_{sbi}")
            t_w2 = S.qs(
                (lambda bb, h_: lambda e: e.dma_start(out=bb[:, :], in_=q2t[h_, :, :]))(
                    b_q2s[sbi], hc),
                deps=[bq2_free[sbi]] + FB, ch=f"bq2_{sbi}")
            pbase1 = 0 if hc % 2 == 0 else 4
            pbase2 = 2 if hc % 2 == 0 else 6
            sig_toks = []
            for half in range(2):
                bk1 = PB[pbase1 + half]
                t_l1 = None
                for kc in range(KC):
                    t_l1 = mm(bk1.ap[:, 0:512],
                              b_q1s[sbi][:, kc * 128:(kc + 1) * 128],
                              b_xn[:, kc * TB + half * 512: kc * TB + (half + 1) * 512],
                              kc == 0, kc == KC - 1,
                              deps=([t_w1] + xn_evs + (bk1.wdeps() if kc == 0 else [])),
                              sig=(kc == KC - 1))
                bk1.wrote(t_l1)
                t_sg = S.ac(
                    (lambda bkap, s_, h_: lambda e: e.activation(
                        b_sig[s_][:, h_ * 512:(h_ + 1) * 512], bkap[:, 0:512],
                        AF.Sigmoid, scale=0.5))(bk1.ap, sbi, half),
                    deps=bk1.rdeps() + bsig_free[sbi], sig=True)
                bk1.read(t_sg)
                sig_toks.append(t_sg)
            bsig_free[sbi] = []
            t_w1_last = t_l1
            for half in range(2):
                bk2 = PB[pbase2 + half]
                t_l2 = None
                for kc in range(KC):
                    t_l2 = mm(bk2.ap[:, 0:512],
                              b_q2s[sbi][:, kc * 128:(kc + 1) * 128],
                              b_xn[:, kc * TB + half * 512: kc * TB + (half + 1) * 512],
                              kc == 0, kc == KC - 1,
                              deps=([t_w2] + xn_evs + (bk2.wdeps() if kc == 0 else [])),
                              sig=(kc == KC - 1))
                bk2.wrote(t_l2)
                t_h = S.dv(
                    (lambda bkap, s_, h_, hc_: lambda e: e.tensor_mul(
                        b_hid[:, hc_ * TB + h_ * 512: hc_ * TB + (h_ + 1) * 512],
                        b_sig[s_][:, h_ * 512:(h_ + 1) * 512],
                        bkap[:, 0:512]))(bk2.ap, sbi, half, hc),
                    deps=bk2.rdeps() + [sig_toks[half]] + bhid_free, sig=True)
                bk2.read(t_h)
                hid_toks.append(t_h)
                bsig_free[sbi].append(t_h)
            bq1_free[sbi] = t_w1_last
            bq2_free[sbi] = t_l2
        bxn_free = [t_l2]

        if bt == 0 and "hid0" in tap_hbm:
            S.qa(lambda e: e.dma_start(out=tap_hbm["hid0"][:, :], in_=b_hid[:, :]),
                 deps=hid_toks, ch='tap')

        last_x3 = None
        for fc in range(KC):
            sbi = fc % 2
            t_w3 = S.qs(
                (lambda bb, f_: lambda e: e.dma_start(out=bb[:, :], in_=q3t[f_, :, :]))(
                    b_q3s[sbi], fc),
                deps=[bq3_free[sbi]] + FB, ch=f"bq3_{sbi}")
            t_sl = S.qs(
                (lambda bb, f_, gg: lambda e: e.dma_start(
                    out=bb[:, :], in_=s_dram[f_ * 128:(f_ + 1) * 128, gg:gg + TB]))(
                    b_s[sbi], fc, g0),
                deps=[bs_free[sbi]] + FB, ch=f"bs{sbi}")
            t_xc = S.qs(
                (lambda bb, bb_, r0, f_: lambda e: e.dma_start(
                    out=bb[:, :].rearrange("p (tb f) -> p tb f", tb=8),
                    in_=x_hbm[bb_, r0:r0 + TB, f_ * 128:(f_ + 1) * 128].rearrange(
                        "(tb p) f -> p tb f", p=128)))(b_xc[sbi], b, t0, fc),
                deps=[bxc_free[sbi]] + FB, ch=f"bxc{sbi}")
            pb0 = 0 if fc % 2 == 0 else 2
            g_toks = []
            for half in range(2):
                bk = PB[pb0 + half]
                t_l3 = None
                for hc in range(HC):
                    t_l3 = mm(bk.ap[:, 0:512],
                              b_q3s[sbi][:, hc * 128:(hc + 1) * 128],
                              b_hid[:, hc * TB + half * 512: hc * TB + (half + 1) * 512],
                              hc == 0, hc == HC - 1,
                              deps=([t_w3] + hid_toks + (bk.wdeps() if hc == 0 else [])),
                              sig=(hc == HC - 1))
                bk.wrote(t_l3)
                last_x3 = t_l3
                t_g = S.ac(
                    (lambda bkap, s_, h_: lambda e: e.activation(
                        b_g[s_][:, h_ * 512:(h_ + 1) * 512], bkap[:, 0:512],
                        AF.Silu, scale=0.25))(bk.ap, sbi, half),
                    deps=bk.rdeps() + bg_free[sbi], sig=True)
                bk.read(t_g)
                g_toks.append(t_g)
            bq3_free[sbi] = last_x3
            t_sgm = S.dv(
                (lambda s_: lambda e: e.tensor_mul(
                    b_sg[s_][:, :], b_g[s_][:, :], b_s[s_][:, :]))(sbi),
                deps=g_toks + [t_sl] + bsg_free[sbi], sig=True)
            bg_free[sbi] = [t_sgm]
            bs_free[sbi] = t_sgm
            st_toks = []
            for tb in range(8):
                bk = PB[4 + tb % 2]
                t_tp = mm(bk.ap[:, 0:128],
                          b_sg[sbi][:, tb * 128:(tb + 1) * 128],
                          c_i128b[:, :], True, True,
                          deps=[t_sgm] + bk.wdeps(), sig=True)
                bk.wrote(t_tp)
                t_st = S.dv(
                    (lambda bkap, s_, tb_, c: lambda e: e.scalar_tensor_tensor(
                        b_os[s_][:, tb_ * 128:(tb_ + 1) * 128],
                        bkap[:, 0:128], st_rstd2[:, c:c + 1],
                        b_xc[s_][:, tb_ * 128:(tb_ + 1) * 128],
                        ALU.mult, ALU.add))(bk.ap, sbi, tb, gtt0 + tb),
                    deps=bk.rdeps() + [t_xc] + bos_free[sbi], sig=True)
                bk.read(t_st)
                st_toks.append(t_st)
            bos_free[sbi] = []
            bsg_free[sbi] = [("pe", S.cnt["pe"])]
            bxc_free[sbi] = st_toks[-1]
            t_od = S.qa(
                (lambda s_, bb_, r0, f_: lambda e: e.dma_start(
                    out=out_hbm[bb_, r0:r0 + TB, f_ * 128:(f_ + 1) * 128].rearrange(
                        "(tb p) f -> p tb f", p=128),
                    in_=b_os[s_][:, :].rearrange("p (tb f) -> p tb f", tb=8)))(
                    sbi, b, t0, fc),
                deps=st_toks, ch=f"out{sbi}")
            bos_free[sbi] = [t_od]
            OUT_TOKS.append(t_od)
        bhid_free = [last_x3]

    fin = [(ch, S.cnt[ch]) for ch in S.cnt if ch.startswith("out")]
    S.emit("qg", lambda e: e.nop(), deps=fin)

    with nc.Block() as block:
        S.replay(block)

    return nc


def _get_nc():
    if "nc" not in _CACHE:
        _CACHE["nc"] = build_program()
    return _CACHE["nc"]


_CACHE = {}


def kernel(**inputs):
    nc = _get_nc()
    lay = _host_layouts(inputs)
    x = np.asarray(inputs["L0_attn_x"], np.float32)
    in_maps = []
    for k in range(NCORE):
        m = {"x": np.ascontiguousarray(x[BPC * k:BPC * (k + 1)])}
        m.update(lay)
        in_maps.append(m)
    res = run_bass_kernel_spmd(nc, in_maps, core_ids=list(range(NCORE)))
    out = np.concatenate([r["out"] for r in res.results], axis=0)
    return out.astype(np.float32)


# version 06
# speedup vs baseline: 1.0497x; 1.0497x over previous
import numpy as np
import ml_dtypes
import concourse.bass as bass
import concourse.mybir as mybir
from concourse.bass_utils import run_bass_kernel_spmd

F32 = mybir.dt.float32
BF16 = mybir.dt.bfloat16
AF = mybir.ActivationFunctionType
ALU = mybir.AluOpType
AX = mybir.AxisListType

D = 2048
H = 16
EH = 128
CH = 64
SEQ = 2048
BPC = 2
TA = 512
NAT = (BPC * SEQ) // TA
TPB_A = SEQ // TA
NTT = TA // 128
NCH = TA // CH
TB = 1024
NBT = (BPC * SEQ) // TB
HC = 44
FH = 5632
KC = 16
NCORE = 8

DEV_SCRATCH_OUT = False
DEV_TAPS = ()
import os as _os
DEV_NAT = int(_os.environ.get("KMK_NAT", NAT))
DEV_NBT = int(_os.environ.get("KMK_NBT", NBT))
DEV_PREP = int(_os.environ.get("KMK_PREP", 1))
DEV_WPASS = int(_os.environ.get("KMK_WPASS", 1))
DEV_STAGE = int(_os.environ.get("KMK_STAGE", 9))
DEV_LD = int(_os.environ.get("KMK_LD", 9))
DEV_LDOP = int(_os.environ.get("KMK_LDOP", 15))


class Sched:
    STREAMS = ("pe", "dv", "ac", "qs", "qg")
    ESEMS = ("pe", "dv", "ac", "qg")

    def __init__(self, nc):
        self.nc = nc
        self.ops = {s: [] for s in self.STREAMS}
        self.sem = {n: nc.alloc_semaphore(f"S_{n}") for n in self.ESEMS}
        self.cnt = {n: 0 for n in self.ESEMS}
        self.lastw = {s: {} for s in self.STREAMS}

    def _dmasem(self, ch):
        if ch not in self.sem:
            self.sem[ch] = self.nc.alloc_semaphore(f"S_{ch}")
            self.cnt[ch] = 0
        return self.sem[ch]

    def _flat(self, deps, out):
        for d in deps:
            if d is None:
                continue
            if isinstance(d, list):
                self._flat(d, out)
            else:
                out.append(d)

    def emit(self, stream, fn, deps=(), sig=None, dma_ch=None):
        flat = []
        self._flat(list(deps), flat)
        need = {}
        for (sname, val) in flat:
            if sname == stream and stream == "pe":
                continue
            if val > need.get(sname, 0):
                need[sname] = val
        waits = []
        for sname, val in need.items():
            cur = self.lastw[stream].get(sname, 0)
            if val > cur:
                waits.append((self.sem[sname], val))
                self.lastw[stream][sname] = val
        tok = None
        inc = None
        if dma_ch is not None:
            self._dmasem(dma_ch)
            self.cnt[dma_ch] += 16
            tok = (dma_ch, self.cnt[dma_ch])
            inc = (self.sem[dma_ch], 16)
        elif sig:
            self.cnt[sig] += 1
            tok = (sig, self.cnt[sig])
            inc = (self.sem[sig], 1)
        self.ops[stream].append((waits, fn, inc))
        return tok

    def pe(self, fn, deps=(), sig=False):
        return self.emit("pe", fn, deps, "pe" if sig else None)

    def dv(self, fn, deps=(), sig=False):
        return self.emit("dv", fn, deps, "dv" if sig else None)

    def ac(self, fn, deps=(), sig=False):
        return self.emit("ac", fn, deps, "ac" if sig else None)

    def qs(self, fn, deps=(), ch="g_qs"):
        return self.emit("qs", fn, deps, dma_ch=ch)

    def qg(self, fn, deps=(), ch="g_qg"):
        return self.emit("qg", fn, deps, dma_ch=ch)

    def gq(self, fn, deps=(), sig=False):
        return self.emit("qg", fn, deps, "qg" if sig else None)

    def qa(self, fn, deps=(), ch="g_qa"):
        return self.emit("ac", fn, deps, dma_ch=ch)

    def fence(self):
        return [(n, self.cnt[n]) for n in self.cnt]

    def replay(self, block):
        S = self

        def mk(stream):
            def run(eng):
                for waits, fn, inc in S.ops[stream]:
                    for (sem, val) in waits:
                        eng.wait_ge(sem, val)
                    ins = fn(eng)
                    if inc is not None:
                        ins.then_inc(*inc)
            return run

        block.tensor(mk("pe"))
        block.vector(mk("dv"))
        block.scalar(mk("ac"))
        block.sync(mk("qs"))
        block.gpsimd(mk("qg"))


class Bank:
    def __init__(self, ap):
        self.ap = ap
        self.ready = None
        self.free = []

    def wdeps(self):
        return list(self.free)

    def wrote(self, tok):
        if tok is not None:
            self.ready = tok
        self.free = []

    def rdeps(self):
        return [self.ready]

    def read(self, tok):
        if tok is not None:
            self.free.append(tok)


def _host_layouts(inp):
    wq = np.asarray(inp["L0_attn_Wq"], np.float32)
    wk = np.asarray(inp["L0_attn_Wk"], np.float32)
    wv = np.asarray(inp["L0_attn_Wv"], np.float32)
    wo = np.asarray(inp["L0_attn_Wo"], np.float32)
    wf = np.asarray(inp["L0_attn_Wf"], np.float32)
    nw = np.asarray(inp["L0_attn_norm_w"], np.float32)
    w1 = np.asarray(inp["L0_ffn_w1"], np.float32)
    w2 = np.asarray(inp["L0_ffn_w2"], np.float32)
    w3 = np.asarray(inp["L0_ffn_w3"], np.float32)

    def fc_major(w):
        return np.ascontiguousarray(
            w.reshape(KC, 128, KC, 128).transpose(2, 1, 0, 3).reshape(KC, 128, D))

    return {
        "wq_l": fc_major(wq),
        "wk_l": fc_major(wk),
        "wv_l": fc_major(wv),
        "wo_l": fc_major(wo),
        "wf_l": np.ascontiguousarray(
            wf.reshape(KC, 128, H).transpose(1, 0, 2).reshape(128, KC * H)),
        "nw_col": np.ascontiguousarray(nw.reshape(KC, 128).T),
        "w1_l": np.ascontiguousarray(
            w1.reshape(HC, 128, KC, 128).transpose(0, 3, 2, 1).reshape(HC, 128, D)),
        "w2_l": np.ascontiguousarray(
            w2.reshape(HC, 128, KC, 128).transpose(0, 3, 2, 1).reshape(HC, 128, D)),
        "w3_l": np.ascontiguousarray(
            w3.reshape(KC, 128, HC, 128).transpose(0, 3, 2, 1).reshape(KC, 128, FH)),
    }


def build_program():
    nc = bass.Bass("TRN2", target_bir_lowering=False, detect_race_conditions=False)

    x_hbm = nc.dram_tensor("x", [BPC, SEQ, D], F32, kind="ExternalInput")
    wq_l = nc.dram_tensor("wq_l", [KC, 128, D], F32, kind="ExternalInput")
    wk_l = nc.dram_tensor("wk_l", [KC, 128, D], F32, kind="ExternalInput")
    wv_l = nc.dram_tensor("wv_l", [KC, 128, D], F32, kind="ExternalInput")
    wo_l = nc.dram_tensor("wo_l", [KC, 128, D], F32, kind="ExternalInput")
    wf_l = nc.dram_tensor("wf_l", [128, KC * H], F32, kind="ExternalInput")
    nw_hbm = nc.dram_tensor("nw_col", [128, KC], F32, kind="ExternalInput")
    w1_l = nc.dram_tensor("w1_l", [HC, 128, D], F32, kind="ExternalInput")
    w2_l = nc.dram_tensor("w2_l", [HC, 128, D], F32, kind="ExternalInput")
    w3_l = nc.dram_tensor("w3_l", [KC, 128, FH], F32, kind="ExternalInput")
    out_hbm = nc.dram_tensor("out", [BPC, SEQ, D], F32, kind="ExternalOutput")

    skind = "ExternalOutput" if DEV_SCRATCH_OUT else "Internal"
    wq_b = nc.dram_tensor("wq_b", [KC, 128, D], BF16)
    wk_b = nc.dram_tensor("wk_b", [KC, 128, D], BF16)
    wv_b = nc.dram_tensor("wv_b", [KC, 128, D], BF16)
    wo_b = nc.dram_tensor("wo_b", [KC, 128, D], BF16)
    q1t = nc.dram_tensor("q1t", [HC, 128, D], BF16, kind=skind)
    q2t = nc.dram_tensor("q2t", [HC, 128, D], BF16, kind=skind)
    q3t = nc.dram_tensor("q3t", [KC, 128, FH], BF16, kind=skind)
    s_dram = nc.dram_tensor("s_dram", [D, BPC * SEQ], BF16, kind=skind)

    TAP_SPECS = {
        "xn0": ([128, H * TA], BF16), "q0": ([128, H * TA], BF16),
        "k0": ([128, H * TA], BF16), "v0": ([128, NTT * D], BF16),
        "ld0": ([16, TA], F32), "cum0": ([16, TA], F32),
        "o0": ([128, H * TA], BF16), "S0": ([128, H * EH], BF16),
        "hid0": ([128, HC * TB], BF16), "xnb0": ([128, KC * TB], BF16),
    }
    tap_hbm = {tn: nc.dram_tensor(f"tap_{tn}", list(TAP_SPECS[tn][0]),
                                  TAP_SPECS[tn][1], kind="ExternalOutput")
               for tn in DEV_TAPS}

    i128b_h = nc.inline_tensor(np.eye(128, dtype=ml_dtypes.bfloat16), "c_i128b")
    i128f_h = nc.inline_tensor(np.eye(128, dtype=np.float32), "c_i128f")
    ones_rf_h = nc.inline_tensor(np.ones((1, 128), np.float32), "c_ones_rf")
    ones_rb_h = nc.inline_tensor(np.ones((1, 128), ml_dtypes.bfloat16), "c_ones_rb")
    ones_cf_h = nc.inline_tensor(np.ones((128, 1), np.float32), "c_ones_cf")
    ones_cb_h = nc.inline_tensor(np.ones((128, 1), ml_dtypes.bfloat16), "c_ones_cb")
    m01 = np.tril(np.ones((CH, CH), np.float32)).T
    mask_np = np.tile(np.tile(m01, (1, H)), (2, 1)).astype(ml_dtypes.bfloat16)
    mask_h = nc.inline_tensor(mask_np, "c_mask")
    CVALS = [-0.5, 0.5, 1.0, 1e-6, 1e-5, 1e-8, 0.0]
    cval_h = nc.inline_tensor(
        np.repeat(np.array([CVALS], np.float32), 128, axis=0), "c_cvals")

    SA = nc.alloc_sbuf_tensor
    c_i128b = SA("s_i128b", [128, 128], BF16)
    c_i128f = SA("s_i128f", [128, 128], F32)
    c_ones_rf = SA("s_ones_rf", [1, 128], F32)
    c_ones_rb = SA("s_ones_rb", [1, 128], BF16)
    c_ones_cf = SA("s_ones_cf", [128, 1], F32)
    c_ones_cb = SA("s_ones_cb", [128, 1], BF16)
    c_mask = SA("s_mask", [128, H * CH], BF16)
    c_cval = SA("s_cvals", [128, len(CVALS)], F32)
    nw_col = SA("s_nw", [128, KC], F32)
    wf_sb = SA("s_wf", [128, KC * H], BF16)
    st_scale = SA("st_scale", [128, 32], F32)
    st_bias = SA("st_bias", [128, 32], F32)
    st_rstd2 = SA("st_rstd2", [128, 32], F32)
    p_acc = SA("p_acc", [128, 4], F32)
    p_tmp = SA("p_tmp", [128, 4], F32)
    p_inv = SA("p_inv", [128, 8], F32)
    p_sc = SA("p_sc", [1, 8], F32)
    a_st = SA("a_st", [128, 8], F32)

    for i, cv in enumerate(CVALS):
        nc.const_aps.register(F32, cv, c_cval[:, i:i + 1])

    base = nc.sbuf_base
    limit = nc.sbuf_top
    cur = [base]

    def place(name, shape, dtype):
        nbytes = int(np.prod(shape[1:])) * mybir.dt.size(dtype)
        off = (cur[0] + 31) & ~31
        cur[0] = off + nbytes
        assert cur[0] <= limit, f"SBUF overflow at {name}: {cur[0]} > {limit}"
        return nc.alloc_sbuf_tensor_at(name, list(shape), dtype, offset=off)

    a_x = [place(f"a_x{i}", [128, D], F32) for i in range(2)]
    a_xno = place("a_xno", [128, H * TA], BF16)
    a_q = place("a_q", [128, H * TA], BF16)
    a_k = place("a_k", [128, H * TA], BF16)
    a_v = place("a_v", [128, NTT * D], BF16)
    a_xt = [place(f"a_xt{i}", [128, D], BF16) for i in range(2)]
    a_vfs = [place(f"a_vfs{i}", [128, TA], BF16) for i in range(2)]
    a_vd = [place(f"a_vd{i}", [128, D], BF16) for i in range(2)]
    a_kt = [place(f"a_kt{i}", [128, D], BF16) for i in range(2)]
    a_wsl = [place(f"a_wsl{i}", [128, D], BF16) for i in range(2)]
    a_S = place("a_S", [128, H * EH], BF16)
    a_q1 = [place(f"a_q1_{i}", [128, H * CH], BF16) for i in range(2)]
    a_cq = [place(f"a_cq{i}", [128, H * CH], BF16) for i in range(2)]
    a_sc = place("a_sc", [128, H * CH], BF16)
    a_crow = [place(f"a_crow{i}", [1, H * 128], BF16) for i in range(2)]
    a_ld = place("a_ld", [16, TA], F32)
    a_cum = place("a_cum", [16, TA], F32)
    a_cfm = place("a_cfm", [16, TA], BF16)
    a_lsg = place("a_lsg", [16, 2 * TA], F32)
    a_lst = [place(f"a_lst{i}", [128, 2 * H], F32) for i in range(2)]
    a_ict = [place(f"a_ict{i}", [128, H], F32) for i in range(2)]
    a_dd = [place(f"a_dd{i}", [128, H], F32) for i in range(2)]
    a_tot = place("a_tot", [1, 4 * H], F32)
    a_eto = place("a_eto", [128, 2 * H], F32)
    a_sq = place("a_sq", [128, 1024], BF16)
    a_ssg = [place(f"a_ssg{i}", [128, TA], BF16) for i in range(2)]
    a_r1 = place("a_r1", [1, 2 * TA], F32)
    p_w = [place(f"p_w{i}", [128, D], F32) for i in range(3)]
    p_g1 = [place(f"p_g1{i}", [128, D], BF16) for i in range(2)]
    p_q = [place(f"p_q{i}", [128, D], BF16) for i in range(2)]
    wps_buf = [place(f"wps_buf{i}", [128, D], BF16) for i in range(2)]
    a_end = cur[0]

    cur[0] = base
    b_xn = place("b_xn", [128, KC * TB], BF16)
    b_hid = place("b_hid", [128, HC * TB], BF16)
    b_q1s = [place(f"b_q1s{i}", [128, D], BF16) for i in range(2)]
    b_q2s = [place(f"b_q2s{i}", [128, D], BF16) for i in range(2)]
    b_q3s = [place(f"b_q3s{i}", [128, FH], BF16) for i in range(2)]
    b_sig = [place(f"b_sig{i}", [128, TB], BF16) for i in range(2)]
    b_g = [place(f"b_g{i}", [128, TB], BF16) for i in range(2)]
    b_s = [place(f"b_s{i}", [128, TB], BF16) for i in range(2)]
    b_sg = [place(f"b_sg{i}", [128, TB], BF16) for i in range(2)]
    b_x = [place("b_x0", [128, D], F32)]
    b_xt = [place("b_xt0", [128, D], BF16)]
    b_xc = [place(f"b_xc{i}", [128, TB], F32) for i in range(2)]
    b_os = [place(f"b_os{i}", [128, TB], F32) for i in range(2)]
    b_end = cur[0]
    assert max(a_end, b_end) <= limit

    S = Sched(nc)
    PB = [Bank(nc.alloc_psum_tensor(f"pb{i}", [128, 512], F32)) for i in range(8)]

    def view(hd, p0, pn, off, dims):
        rl = hd.shape[-1]
        return bass.AP(hd, p0 * rl + off, [[rl, pn]] + dims)

    CONST = []
    for dst, src in [(c_i128b, i128b_h), (c_i128f, i128f_h), (c_ones_rf, ones_rf_h),
                     (c_ones_rb, ones_rb_h), (c_ones_cf, ones_cf_h),
                     (c_ones_cb, ones_cb_h), (c_mask, mask_h),
                     (c_cval, cval_h), (nw_col, nw_hbm)]:
        CONST.append(S.qs((lambda d, s_: lambda e: e.dma_start(out=d[:, :], in_=s_[:, :]))(dst, src), ch="const"))
    CONST.append(S.qg(lambda e: e.dma_start(out=wf_sb[:, :], in_=wf_l[:, :]), ch="constg"))

    wstore = {}
    WSL_STATE = {"free": [list(CONST), list(CONST)], "i": 0}

    def emit_wpass():
        wp_free = [list(CONST), list(CONST)]
        pi = 0
        for wi, (srcw, dstw) in enumerate(
                [(wq_l, wq_b), (wk_l, wk_b), (wv_l, wv_b), (wo_l, wo_b)] if DEV_WPASS else []):
            for fc in range(KC):
                bi = pi % 2
                pi += 1
                t_in = S.qg(
                    (lambda bb, s_, f_: lambda e: e.dma_start(out=bb[:, :], in_=s_[f_, :, :]))(
                        wps_buf[bi], srcw, fc),
                    deps=wp_free[bi], ch=f"wpl{bi}")
                t_out = S.qs(
                    (lambda bb, d_, f_: lambda e: e.dma_start(out=d_[f_, :, :], in_=bb[:, :]))(
                        wps_buf[bi], dstw, fc),
                    deps=[t_in], ch=f"wps{bi}")
                wp_free[bi] = [t_out]
                wstore[(wi, fc)] = t_out

    def _mk_units(srcs):
        units = []
        for mi, src, nch, width, dst in srcs:
            for ci in range(nch):
                if width == D:
                    units.append((mi, src, dst, ci, 0, D))
                else:
                    for (o, w_) in ((0, 2048), (2048, 2048), (4096, 1536)):
                        units.append((mi, src, dst, ci, o, w_))
        return units

    G_UNITS = _mk_units([(0, w1_l, HC, D, None), (1, w2_l, HC, D, None),
                         (2, w3_l, KC, FH, None)])
    Q_UNITS = _mk_units([(0, w1_l, HC, D, q1t), (1, w2_l, HC, D, q2t),
                         (2, w3_l, KC, FH, q3t)])
    PS = {"gi": 0, "qi": 0, "bfree": [list(CONST) for _ in range(3)],
          "qfree": [None] * 2, "accin": [False] * 3, "gdone": None, "inv": None,
          "s1free": [None, None]}

    def prep_gamma_step():
        if PS["gi"] >= len(G_UNITS):
            return
        mi, src, _, ci, off, w_ = G_UNITS[PS["gi"]]
        bi = PS["gi"] % 3
        PS["gi"] += 1
        buf = p_w[bi]
        t_in = S.qs(
            (lambda bb, s_, c_, o_, ww: lambda e: e.dma_start(
                out=bb[:, 0:ww], in_=s_[c_, :, o_:o_ + ww]))(buf, src, ci, off, w_),
            deps=PS["bfree"][bi], ch=f"pw{bi}")
        t_red = S.dv(
            (lambda bb, ww: lambda e: e.tensor_reduce(
                p_tmp[:, 3:4], bb[:, 0:ww], AX.X, ALU.add, apply_absolute_value=True))(buf, w_),
            deps=[t_in], sig=True)
        PS["bfree"][bi] = [t_red]
        if not PS["accin"][mi]:
            PS["accin"][mi] = True
            t_acc = S.gq(
                (lambda m_: lambda e: e.tensor_copy(p_acc[:, m_:m_ + 1], p_tmp[:, 3:4]))(mi),
                deps=[t_red], sig=True)
        else:
            t_acc = S.gq(
                (lambda m_: lambda e: e.tensor_add(
                    p_acc[:, m_:m_ + 1], p_acc[:, m_:m_ + 1], p_tmp[:, 3:4]))(mi),
                deps=[t_red], sig=True)
        PS["gdone"] = t_acc

    def prep_gamma_finalize():
        bank = PB[7]
        toks = []
        for mi in range(3):
            nelem = FH * D
            t_mm = S.pe(
                (lambda m_: lambda e: e.matmul(
                    bank.ap[0:1, 4 * m_:4 * m_ + 1], p_acc[:, m_:m_ + 1],
                    c_ones_cf[:, :], start=True, stop=True))(mi),
                deps=[PS["gdone"]] + bank.wdeps(), sig=True)
            bank.wrote(t_mm)
            t_c = S.dv(
                (lambda m_, n_: lambda e: e.tensor_scalar(
                    p_sc[0:1, m_:m_ + 1], bank.ap[0:1, 4 * m_:4 * m_ + 1],
                    1.0 / n_, 1e-8, ALU.mult, ALU.add))(mi, nelem),
                deps=bank.rdeps(), sig=True)
            bank.read(t_c)
            t_r = S.dv(
                (lambda m_: lambda e: e.reciprocal(
                    p_sc[0:1, 4 + m_:5 + m_], p_sc[0:1, m_:m_ + 1]))(mi),
                deps=[t_c], sig=True)
            t_b = S.pe(
                (lambda m_: lambda e: e.matmul(
                    bank.ap[:, 8 + m_:9 + m_], c_ones_rf[:, :],
                    p_sc[0:1, 4 + m_:5 + m_], start=True, stop=True))(mi),
                deps=[t_r] + bank.wdeps(), sig=True)
            bank.wrote(t_b)
            t_e = S.dv(
                (lambda m_: lambda e: e.tensor_copy(
                    p_inv[:, m_:m_ + 1], bank.ap[:, 8 + m_:9 + m_]))(mi),
                deps=bank.rdeps(), sig=True)
            bank.read(t_e)
            t_n = S.dv(
                (lambda m_: lambda e: e.tensor_scalar(
                    p_inv[:, 4 + m_:5 + m_], p_inv[:, m_:m_ + 1], -1.0, None, ALU.mult))(mi),
                deps=[t_e], sig=True)
            toks.append(t_n)
        PS["inv"] = toks

    def prep_quant_step():
        if PS["qi"] >= len(Q_UNITS):
            return
        mi, src, dst, ci, off, w_ = Q_UNITS[PS["qi"]]
        bi = PS["qi"] % 2
        PS["qi"] += 1
        buf = p_w[bi]
        qb = p_q[bi]
        t_in = S.qs(
            (lambda bb, s_, c_, o_, ww: lambda e: e.dma_start(
                out=bb[:, 0:ww], in_=s_[c_, :, o_:o_ + ww]))(buf, src, ci, off, w_),
            deps=PS["bfree"][bi] + PS["inv"], ch=f"pw{bi}")
        gi2 = PS["qi"] % 2
        invc = p_inv[:, mi:mi + 1]
        ninvc = p_inv[:, 4 + mi:5 + mi]
        t_s1 = S.dv(
            (lambda bb, ww, iv, g_: lambda e: e.tensor_scalar(
                g_[:, 0:ww], bb[:, 0:ww], iv, 0.5, ALU.mult, ALU.is_ge))(
                buf, w_, invc, p_g1[gi2]),
            deps=[t_in, PS["s1free"][gi2]], sig=True)
        t_s2 = S.dv(
            (lambda bb, ww, iv, q_: lambda e: e.tensor_scalar(
                q_[:, 0:ww], bb[:, 0:ww], iv, 0.5, ALU.mult, ALU.is_ge))(
                buf, w_, ninvc, qb),
            deps=[t_in, PS["qfree"][bi]], sig=True)
        PS["bfree"][bi] = [t_s2]
        t_add = S.gq(
            (lambda ww, q_, g_: lambda e: e.tensor_sub(
                q_[:, 0:ww], g_[:, 0:ww], q_[:, 0:ww]))(w_, qb, p_g1[gi2]),
            deps=[t_s1, t_s2], sig=True)
        PS["s1free"][gi2] = t_add
        t_out = S.qs(
            (lambda q_, d_, c_, o_, ww: lambda e: e.dma_start(
                out=d_[c_, :, o_:o_ + ww], in_=q_[:, 0:ww]))(qb, dst, ci, off, w_),
            deps=[t_add], ch=f"pq{bi}")
        PS["qfree"][bi] = t_out

    def prep_steps(n, quant):
        if not DEV_PREP:
            return
        for _ in range(n):
            (prep_quant_step if quant else prep_gamma_step)()

    def mm(out_ap, l_ap, r_ap, start, stop, deps=(), sig=False):
        return S.pe(
            (lambda o_, l_, r_, s0, s1: lambda e: e.matmul(o_, l_, r_, start=s0, stop=s1))(
                out_ap, l_ap, r_ap, start, stop),
            deps, sig)

    def load_wslab(wi, fc, at, deps_extra=()):
        bi = WSL_STATE["i"] % 2
        WSL_STATE["i"] += 1
        srcs_f32 = [wq_l, wk_l, wv_l, wo_l]
        srcs_b16 = [wq_b, wk_b, wv_b, wo_b]
        deps = WSL_STATE["free"][bi] + list(deps_extra)
        if at <= 1 or not DEV_WPASS:
            tok = S.qg(
                (lambda bb, s_, f_: lambda e: e.dma_start(out=bb[:, :], in_=s_[f_, :, :]))(
                    a_wsl[bi], srcs_f32[wi], fc), deps=deps, ch=f"wslg{bi}")
        else:
            tok = S.qs(
                (lambda bb, s_, f_: lambda e: e.dma_start(out=bb[:, :], in_=s_[f_, :, :]))(
                    a_wsl[bi], srcs_b16[wi], fc),
                deps=deps + [wstore[(wi, fc)]], ch=f"wsl{bi}")
        return bi, tok

    def wsl_release(bi, toks):
        WSL_STATE["free"][bi] = list(toks)

    x_free = [[], []]
    xt_free = [[], []]
    vfs_free = [[], []]
    vd_free = [[], []]
    kt_free = [[], []]
    q1_free = [[], []]
    cq_free = [[], []]
    crow_free = [[], []]
    lst_free = [[], []]
    ict_free = [[], []]
    eto_free = [[], []]
    dd_free = [[], []]
    sc_free = [[], []]
    ssg_free = [[], []]
    sq_free = [[], []]
    S_write = [None] * H
    ld_free = []
    xno_writable = []
    xno_free = []
    q_free, k_free, v_free = [], [], []

    for at in range(DEV_NAT):
        b = at // TPB_A
        t0 = (at % TPB_A) * TA
        g0 = at * TA
        first_of_seq = (at % TPB_A) == 0

        xn_evac_toks = []
        for tt in range(NTT):
            bi = tt % 2
            ttg = g0 // 128 + tt
            t_x = S.qs(
                (lambda b_, bb, r0: lambda e: e.dma_start(
                    out=a_x[b_][:, :], in_=x_hbm[bb, r0:r0 + 128, :]))(bi, b, t0 + tt * 128),
                deps=x_free[bi] + CONST, ch=f"x{bi}")
            t_sq = S.ac(
                (lambda b_, xb: lambda e: e.activation(
                    xb[:, :], a_x[b_][:, :], AF.Square, accum_out=a_st[:, 1:2]))(bi, a_xt[bi]),
                deps=[t_x] + xt_free[bi], sig=True)
            t_sm = S.dv(
                (lambda b_, xb: lambda e: e.tensor_scalar(
                    xb[:, :], a_x[b_][:, :], 1.0, 0.0, ALU.mult, ALU.add,
                    accum_out=a_st[:, 0:1]))(bi, a_xt[bi]),
                deps=[t_sq], sig=True)
            t_mu = S.dv(
                lambda e: e.tensor_scalar(a_st[:, 3:4], a_st[:, 0:1], 1.0 / D, None, ALU.mult),
                deps=[t_sm], sig=True)
            t_e2 = S.dv(
                lambda e: e.tensor_scalar(a_st[:, 1:2], a_st[:, 1:2], 1.0 / D, None, ALU.mult),
                deps=[t_sq], sig=True)
            t_nv = S.dv(
                lambda e: e.scalar_tensor_tensor(
                    a_st[:, 2:3], a_st[:, 3:4], a_st[:, 3:4], a_st[:, 1:2],
                    ALU.mult, ALU.subtract),
                deps=[t_mu, t_e2], sig=True)
            t_ln = S.ac(
                lambda e: e.activation(a_st[:, 4:5], a_st[:, 2:3], AF.Ln,
                                       scale=-1.0, bias=1e-5),
                deps=[t_nv], sig=True)
            t_rs = S.ac(
                (lambda c: lambda e: e.activation(
                    st_scale[:, c:c + 1], a_st[:, 4:5], AF.Exp, scale=-0.5))(ttg),
                deps=[t_ln], sig=True)
            t_nb = S.dv(
                (lambda c: lambda e: e.tensor_scalar(
                    st_bias[:, c:c + 1], st_scale[:, c:c + 1], a_st[:, 3:4], -1.0,
                    ALU.mult, ALU.mult))(ttg),
                deps=[t_rs], sig=True)
            t_ap = S.ac(
                (lambda b_, xb, c: lambda e: e.activation(
                    xb[:, :], a_x[b_][:, :], AF.Identity,
                    scale=st_scale[:, c:c + 1], bias=st_bias[:, c:c + 1]))(bi, a_xt[bi], ttg),
                deps=[t_nb, t_sm], sig=True)
            x_free[bi] = [t_ap, t_sm]
            grp_ev = []
            for g in range(4):
                bk = PB[4 + g % 2]
                t_last = None
                for j in range(4):
                    fcb = g * 4 + j
                    t_last = mm(
                        bk.ap[:, j * 128:(j + 1) * 128],
                        a_xt[bi][:, fcb * 128:(fcb + 1) * 128],
                        c_i128b[:, :], True, True,
                        deps=[t_ap] + (bk.wdeps() if j == 0 else []),
                        sig=(j == 3))
                bk.wrote(t_last)
                t_ev = S.ac(
                    (lambda bkap, g_, tt_: lambda e: e.copy(
                        view(a_xno, 0, 128, (g_ * 4) * TA + tt_ * 128, [[TA, 4], [1, 128]]),
                        bkap[:, :].rearrange("p (a b) -> p a b", a=4)))(bk.ap, g, tt),
                    deps=bk.rdeps() + xno_free, sig=True)
                bk.read(t_ev)
                grp_ev.append(t_ev)
            xt_free[bi] = grp_ev
            xn_evac_toks.extend(grp_ev)

        if DEV_STAGE < 2:
            continue
        bk6 = PB[6]
        t_last = None
        for kc in range(KC):
            t_last = mm(bk6.ap[0:16, 0:512], wf_sb[:, kc * H:(kc + 1) * H],
                        a_xno[:, kc * TA:(kc + 1) * TA], kc == 0, kc == KC - 1,
                        deps=xn_evac_toks + (bk6.wdeps() if kc == 0 else []),
                        sig=(kc == KC - 1))
        bk6.wrote(t_last)
        t_abs = S.ac(lambda e: e.activation(a_lsg[:, 0:TA], bk6.ap[0:16, 0:512], AF.Abs),
                     deps=bk6.rdeps() + ld_free, sig=True)
        if DEV_LD < 2:
            bk6.read(t_abs)
            continue
        t_exp = t_abs
        t_ln1 = t_abs
        if DEV_LDOP & 1:
            t_exp = S.ac(lambda e: e.activation(a_lsg[:, TA:2 * TA], a_lsg[:, 0:TA],
                                                AF.Exp, scale=-1.0), deps=[t_abs], sig=True)
        if DEV_LDOP & 2:
            t_ln1 = S.ac(lambda e: e.activation(a_lsg[:, 0:TA], a_lsg[:, TA:2 * TA],
                                                AF.Ln, bias=c_cval[0:16, 2:3]),
                         deps=[t_exp] + CONST, sig=True)
        t_min = t_abs
        if DEV_LDOP & 4:
            t_min = S.dv(lambda e: e.tensor_scalar(a_ld[:, :], bk6.ap[0:16, 0:512],
                                                   0.0, None, ALU.min),
                         deps=bk6.rdeps() + ld_free + [t_abs], sig=True)
            bk6.read(t_min)
        bk6.read(t_abs)
        t_ld = t_min
        if DEV_LDOP & 8:
            t_ld = S.dv(lambda e: e.tensor_sub(a_ld[:, :], a_ld[:, :], a_lsg[:, 0:TA]),
                        deps=[t_ln1, t_min], sig=True)
        if DEV_LD < 3:
            continue
        t_cums = []
        for c in range(NCH):
            t_cums.append(S.dv(
                (lambda c_: lambda e: e.tensor_tensor_scan(
                    a_cum[:, c_ * CH:(c_ + 1) * CH], a_ld[:, c_ * CH:(c_ + 1) * CH],
                    a_ld[:, c_ * CH:(c_ + 1) * CH], 0.0, ALU.add, ALU.bypass))(c),
                deps=[t_ld] + ld_free, sig=True))
        t_cfm = S.ac(lambda e: e.activation(a_cfm[:, :], a_cum[:, :], AF.Exp),
                     deps=t_cums + ld_free, sig=True)
        if at == 0 and "ld0" in tap_hbm:
            S.qa((lambda: lambda e: e.dma_start(out=tap_hbm["ld0"][:, :], in_=a_ld[:, :]))(),
                 deps=[t_ld])
        if at == 0 and "cum0" in tap_hbm:
            S.qa((lambda: lambda e: e.dma_start(out=tap_hbm["cum0"][:, :], in_=a_cum[:, :]))(),
                 deps=t_cums)

        if DEV_STAGE < 3:
            continue
        def project(wi, dst, act, dst_free, v_mode=False):
            evs = []
            for fc in range(KC):
                sbi, t_w = load_wslab(wi, fc, at)
                bk = PB[fc % 4]
                t_last = None
                for kc in range(KC):
                    t_last = mm(bk.ap[:, 0:512],
                                a_wsl[sbi][:, kc * 128:(kc + 1) * 128],
                                a_xno[:, kc * TA:(kc + 1) * TA],
                                kc == 0, kc == KC - 1,
                                deps=([t_w] + xn_evac_toks +
                                      (bk.wdeps() if kc == 0 else [])),
                                sig=(kc == KC - 1))
                bk.wrote(t_last)
                wsl_release(sbi, [t_last])
                if not v_mode:
                    t_ev = S.ac(
                        (lambda bkap, f_: lambda e: e.activation(
                            dst[:, f_ * TA:(f_ + 1) * TA], bkap[:, 0:512], act))(bk.ap, fc),
                        deps=bk.rdeps() + dst_free, sig=True)
                    bk.read(t_ev)
                    evs.append(t_ev)
                else:
                    vb = fc % 2
                    t_ev = S.ac(
                        (lambda bkap, v_: lambda e: e.copy(a_vfs[v_][:, :], bkap[:, 0:512]))(
                            bk.ap, vb),
                        deps=bk.rdeps() + vfs_free[vb], sig=True)
                    bk.read(t_ev)
                    tbk = PB[4 + fc % 2]
                    t_tl = None
                    for tb in range(4):
                        t_tl = mm(tbk.ap[:, tb * 128:(tb + 1) * 128],
                                  a_vfs[vb][:, tb * 128:(tb + 1) * 128],
                                  c_i128b[:, :], True, True,
                                  deps=[t_ev] + (tbk.wdeps() if tb == 0 else []),
                                  sig=(tb == 3))
                    tbk.wrote(t_tl)
                    t_vv = S.dv(
                        (lambda bkap, f_: lambda e: e.tensor_copy(
                            view(a_v, 0, 128, f_ * 128, [[D, 4], [1, 128]]),
                            bkap[:, :].rearrange("p (a b) -> p a b", a=4)))(tbk.ap, fc),
                        deps=tbk.rdeps() + dst_free, sig=True)
                    tbk.read(t_vv)
                    vfs_free[vb] = [t_vv]
                    evs.append(t_vv)
            return evs

        q_evs = project(0, a_q, AF.Silu, q_free)
        k_evs = project(1, a_k, AF.Silu, k_free)
        v_evs = project(2, None, None, v_free, v_mode=True)

        if at == 0 and "xn0" in tap_hbm:
            S.qa(lambda e: e.dma_start(out=tap_hbm["xn0"][:, :], in_=a_xno[:, :]),
                 deps=xn_evac_toks, ch='tap')
        if at == 0 and "q0" in tap_hbm:
            S.qa(lambda e: e.dma_start(out=tap_hbm["q0"][:, :], in_=a_q[:, :]), deps=q_evs)
        if at == 0 and "k0" in tap_hbm:
            S.qa(lambda e: e.dma_start(out=tap_hbm["k0"][:, :], in_=a_k[:, :]), deps=k_evs)
        if at == 0 and "v0" in tap_hbm:
            S.qa(lambda e: e.dma_start(out=tap_hbm["v0"][:, :], in_=a_v[:, :]), deps=v_evs)

        new_q_free, new_k_free, new_v_free = [], [], []
        new_ld_free = []
        o_ev_all = []
        if DEV_STAGE < 4:
            continue

        for tt in range(NTT):
            tp = tt % 2
            bk7 = PB[7]
            t_t1 = mm(bk7.ap[:, 0:16], a_ld[:, tt * 128:(tt + 1) * 128],
                      c_i128f[0:16, 0:16], True, True,
                      deps=[t_ld] + bk7.wdeps(), sig=False)
            t_t2 = mm(bk7.ap[:, 16:32], a_cum[:, tt * 128:(tt + 1) * 128],
                      c_i128f[0:16, 0:16], True, True,
                      deps=t_cums, sig=True)
            bk7.wrote(t_t2)
            new_ld_free.append(t_t2)
            t_le = S.ac(
                (lambda bkap, t_: lambda e: e.copy(a_lst[t_][:, :], bkap[:, 0:32]))(
                    bk7.ap, tp),
                deps=bk7.rdeps() + lst_free[tp], sig=True)
            bk7.read(t_le)
            t_ic = S.ac(
                (lambda t_: lambda e: e.activation(
                    a_ict[t_][:, :], a_lst[t_][:, 16:32], AF.Exp, scale=-1.0))(tp),
                deps=[t_le] + ict_free[tp], sig=True)
            ict_free[tp] = []
            lst_free[tp] = []
            t_crows = []
            for h in range(H):
                t_crows.append(S.qa(
                    (lambda t_, h_, tt_: lambda e: e.dma_start(
                        out=a_crow[t_][0:1, h_ * 128:(h_ + 1) * 128],
                        in_=a_cfm[h_:h_ + 1, tt_ * 128:(tt_ + 1) * 128]))(tp, h, tt),
                    deps=[t_cfm] + crow_free[tp], ch=f"crow{tp}"))
            crow_free[tp] = []
            new_ld_free.extend(t_crows[-1:])
            kt_evs = []
            for g in range(4):
                bk = PB[2 + g % 2]
                t_tl = None
                for j in range(4):
                    h = g * 4 + j
                    t_tl = mm(bk.ap[:, j * 128:(j + 1) * 128],
                              a_k[:, h * TA + tt * 128: h * TA + (tt + 1) * 128],
                              c_i128b[:, :], True, True,
                              deps=[k_evs[h]] + (bk.wdeps() if j == 0 else []),
                              sig=(j == 3))
                bk.wrote(t_tl)
                t_ke = S.dv(
                    (lambda bkap, t_, g_: lambda e: e.tensor_copy(
                        a_kt[t_][:, g_ * 512:(g_ + 1) * 512], bkap[:, :]))(bk.ap, tp, g),
                    deps=bk.rdeps() + kt_free[tp], sig=True)
                bk.read(t_ke)
                kt_evs.append(t_ke)
            kt_free[tp] = []

            if DEV_STAGE < 5:
                continue
            for half in range(2):
                c = 2 * tt + half
                p0 = 64 * half
                pr = c % 2
                bk6 = PB[6]
                t_tr = mm(bk6.ap[0:1, 0:16], c_ones_cf[p0:p0 + 64, :],
                          a_lst[tp][p0:p0 + 64, 0:16], True, True,
                          deps=[t_le] + bk6.wdeps(), sig=True)
                bk6.wrote(t_tr)
                t_te = S.ac(
                    (lambda bkap, pr_: lambda e: e.copy(
                        a_tot[0:1, pr_ * 16:(pr_ + 1) * 16], bkap[0:1, 0:16]))(bk6.ap, pr),
                    deps=bk6.rdeps(), sig=True)
                bk6.read(t_te)
                t_ee = S.ac(
                    (lambda pr_: lambda e: e.activation(
                        a_tot[0:1, 32 + pr_ * 16:32 + (pr_ + 1) * 16],
                        a_tot[0:1, pr_ * 16:(pr_ + 1) * 16], AF.Exp))(pr),
                    deps=[t_te], sig=True)
                t_tb = mm(bk6.ap[:, 16:32], c_ones_rf[:, :],
                          a_tot[0:1, pr * 16:(pr + 1) * 16], True, True,
                          deps=[t_te] + bk6.wdeps(), sig=True)
                bk6.wrote(t_tb)
                t_dd = S.dv(
                    (lambda bkap, t_, q0_: lambda e: e.tensor_sub(
                        a_dd[t_][q0_:q0_ + 64, :], bkap[q0_:q0_ + 64, 16:32],
                        a_lst[t_][q0_:q0_ + 64, 16:32]))(bk6.ap, tp, p0),
                    deps=bk6.rdeps() + [t_le] + dd_free[tp], sig=True)
                bk6.read(t_dd)
                t_de = S.ac(
                    (lambda t_, q0_: lambda e: e.activation(
                        a_dd[t_][q0_:q0_ + 64, :], a_dd[t_][q0_:q0_ + 64, :], AF.Exp))(tp, p0),
                    deps=[t_dd], sig=True)
                t_eb = mm(bk6.ap[:, 32:48], c_ones_rf[:, :],
                          a_tot[0:1, 32 + pr * 16:32 + (pr + 1) * 16], True, True,
                          deps=[t_ee] + bk6.wdeps(), sig=True)
                bk6.wrote(t_eb)
                t_ev = S.ac(
                    (lambda bkap, pr_: lambda e: e.copy(
                        a_eto[:, pr_ * 16:(pr_ + 1) * 16], bkap[:, 32:48]))(bk6.ap, pr),
                    deps=bk6.rdeps() + eto_free[pr], sig=True)
                bk6.read(t_ev)
                eto_free[pr] = []
                t_vd = S.dv(
                    (lambda t_, q0_, tt_: lambda e: e.tensor_mul(
                        view(a_vd[t_], q0_, 64, 0, [[128, H], [1, 128]]),
                        view(a_v, q0_, 64, tt_ * D, [[128, H], [1, 128]]),
                        view(a_dd[t_], q0_, 64, 0, [[1, H], [0, 128]])))(tp, p0, tt),
                    deps=[t_de] + v_evs + vd_free[tp], sig=True)
                vd_free[tp] = []
                cq_toks = []
                for rnd in range(2):
                    t_cb = mm(bk7.ap[:, 0:512], c_ones_rb[:, :],
                              bass.AP(a_crow[tp], rnd * 1024 + half * 64,
                                      [[2048, 1], [128, 8], [1, 64]]),
                              True, True,
                              deps=t_crows + bk7.wdeps(), sig=True)
                    bk7.wrote(t_cb)
                    t_ce = S.ac(
                        (lambda bkap, pr_, r_: lambda e: e.copy(
                            a_cq[pr_][:, r_ * 512:(r_ + 1) * 512], bkap[:, 0:512]))(
                            bk7.ap, pr, rnd),
                        deps=bk7.rdeps() + cq_free[pr], sig=True)
                    bk7.read(t_ce)
                    cq_toks.append(t_ce)
                cq_free[pr] = []
                t_q1 = S.dv(
                    (lambda pr_, c_: lambda e: e.tensor_mul(
                        view(a_q1[pr_], 0, 128, 0, [[CH, H], [1, CH]]),
                        view(a_q, 0, 128, c_ * CH, [[TA, H], [1, CH]]),
                        view(a_cq[pr_], 0, 128, 0, [[CH, H], [1, CH]])))(pr, c),
                    deps=cq_toks + q_evs + q1_free[pr], sig=True)
                q1_free[pr] = []
                for g in range(2):
                    bk = PB[4 + g]
                    for j in range(8):
                        h = g * 8 + j
                        mm(bk.ap[p0:p0 + 64, j * 64:(j + 1) * 64],
                           a_k[:, h * TA + c * CH: h * TA + (c + 1) * CH],
                           a_q1[pr][:, h * CH:(h + 1) * CH],
                           True, True,
                           deps=([t_q1, k_evs[h]] + (bk.wdeps() if j == 0 else [])),
                           sig=(j == 7))
                    tok = ("pe", S.cnt["pe"])
                    bk.wrote(tok)
                sc_toks = []
                for g in range(2):
                    bk = PB[4 + g]
                    t_s1 = S.dv(
                        (lambda bkap, q0_, g_, t_: lambda e: e.tensor_mul(
                            view(a_sc, q0_, 64, g_ * 512, [[CH, 8], [1, CH]]),
                            bkap[q0_:q0_ + 64, :].rearrange("p (h i) -> p h i", h=8),
                            view(a_ict[t_], q0_, 64, g_ * 8, [[1, 8], [0, CH]])))(
                            bk.ap, p0, g, tp),
                        deps=bk.rdeps() + [t_ic] + sc_free[pr], sig=True)
                    bk.read(t_s1)
                    t_s2 = S.dv(
                        (lambda q0_, g_: lambda e: e.tensor_mul(
                            a_sc[q0_:q0_ + 64, g_ * 512:(g_ + 1) * 512],
                            a_sc[q0_:q0_ + 64, g_ * 512:(g_ + 1) * 512],
                            c_mask[q0_:q0_ + 64, g_ * 512:(g_ + 1) * 512]))(p0, g),
                        deps=[t_s1], sig=True)
                    sc_toks.append(t_s2)
                    ict_free[tp].append(t_s1)
                sc_free[pr] = []
                skip_inter = (first_of_seq and c == 0)
                for h in range(H):
                    go = h // 8
                    bko = PB[go]
                    first_in_bank = (h % 8 == 0)
                    deps_o = ([sc_toks[go]] + v_evs +
                              (bko.wdeps() if first_in_bank else []))
                    mm(bko.ap[:, (h % 8) * 64:(h % 8 + 1) * 64],
                       view(a_v, p0, 64, tt * D + h * 128, [[1, 128]]),
                       a_sc[p0:p0 + 64, h * 64:(h + 1) * 64],
                       True, skip_inter, deps=deps_o,
                       sig=(skip_inter and h % 8 == 7))
                    if not skip_inter:
                        mm(bko.ap[:, (h % 8) * 64:(h % 8 + 1) * 64],
                           a_S[:, h * 128:(h + 1) * 128],
                           a_q1[pr][:, h * CH:(h + 1) * CH],
                           False, True, deps=[S_write[h], t_q1],
                           sig=(h % 8 == 7))
                    if h % 8 == 7:
                        PB[go].wrote(("pe", S.cnt["pe"]))
                o_evs = []
                for g in range(2):
                    bk = PB[g]
                    t_oe = S.ac(
                        (lambda bkap, g_, c_: lambda e: e.copy(
                            view(a_xno, 0, 128, (g_ * 8) * TA + c_ * CH, [[TA, 8], [1, CH]]),
                            bkap[:, :].rearrange("p (h i) -> p h i", h=8)))(bk.ap, g, c),
                        deps=bk.rdeps(), sig=True)
                    bk.read(t_oe)
                    o_evs.append(t_oe)
                o_ev_all.extend(o_evs)
                for g4 in range(4):
                    bku = PB[2 + g4 % 2]
                    t_uu = None
                    for j in range(4):
                        h = g4 * 4 + j
                        t_uu = mm(bku.ap[:, j * 128:(j + 1) * 128],
                                  view(a_kt[tp], p0, 64, h * 128, [[1, 128]]),
                                  view(a_vd[tp], p0, 64, h * 128, [[1, 128]]),
                                  True, True,
                                  deps=([kt_evs[h // 4], t_vd] +
                                        (bku.wdeps() if j == 0 else [])),
                                  sig=(j == 3))
                    bku.wrote(t_uu)
                    for j in range(4):
                        h = g4 * 4 + j
                        if skip_inter:
                            t_su = S.dv(
                                (lambda bkap, h_, j_: lambda e: e.tensor_copy(
                                    a_S[:, h_ * 128:(h_ + 1) * 128],
                                    bkap[:, j_ * 128:(j_ + 1) * 128]))(bku.ap, h, j),
                                deps=bku.rdeps() + o_evs, sig=True)
                        else:
                            t_su = S.dv(
                                (lambda bkap, h_, j_, pr_: lambda e: e.scalar_tensor_tensor(
                                    a_S[:, h_ * 128:(h_ + 1) * 128],
                                    a_S[:, h_ * 128:(h_ + 1) * 128],
                                    a_eto[:, pr_ * 16 + h_:pr_ * 16 + h_ + 1],
                                    bkap[:, j_ * 128:(j_ + 1) * 128],
                                    ALU.mult, ALU.add))(bku.ap, h, j, pr),
                                deps=bku.rdeps() + [t_ev] + o_evs, sig=True)
                        S_write[h] = t_su
                        bku.read(t_su)
                        eto_free[pr].append(t_su)
                q1_free[pr] = [("pe", S.cnt["pe"])]
                cq_free[pr] = [t_q1]
                sc_free[pr] = [("pe", S.cnt["pe"])]
                dd_free[tp] = [t_vd]
                vd_free[tp] = [S_write[H - 1]]
                kt_free[tp] = [S_write[H - 1]]
                crow_free[tp] = [("pe", S.cnt["pe"])]
                lst_free[tp] = [t_dd, t_tr]
            prep_steps(9, quant=(at >= NAT // 2))

        if at == NAT // 2 - 1:
            while PS["gi"] < len(G_UNITS):
                prep_gamma_step()
            prep_gamma_finalize()

        if at == 0 and "S0" in tap_hbm:
            S.qa(lambda e: e.dma_start(out=tap_hbm["S0"][:, :], in_=a_S[:, :]),
                 deps=list(S_write, ch='tap'))
        if at == 0 and "o0" in tap_hbm:
            S.qa(lambda e: e.dma_start(out=tap_hbm["o0"][:, :], in_=a_xno[:, :]),
                 deps=o_ev_all, ch='tap')

        if DEV_STAGE < 6:
            continue
        bk6 = PB[6]
        nw_toks = []
        t_ssq_last = None
        for fc in range(KC):
            sqh = fc % 2
            t_sq = S.ac(
                (lambda f_, s_: lambda e: e.activation(
                    a_sq[:, s_ * 512:(s_ + 1) * 512],
                    a_xno[:, f_ * TA:(f_ + 1) * TA], AF.Square))(fc, sqh),
                deps=o_ev_all + sq_free[sqh], sig=True)
            t_mmq = mm(bk6.ap[0:1, 0:512], c_ones_cb[:, :],
                       a_sq[:, sqh * 512:(sqh + 1) * 512],
                       fc == 0, fc == KC - 1,
                       deps=[t_sq] + (bk6.wdeps() if fc == 0 else []),
                       sig=True)
            sq_free[sqh] = [t_mmq]
            t_nw = S.dv(
                (lambda f_: lambda e: e.tensor_scalar(
                    a_xno[:, f_ * TA:(f_ + 1) * TA],
                    a_xno[:, f_ * TA:(f_ + 1) * TA],
                    nw_col[:, f_:f_ + 1], None, ALU.mult))(fc),
                deps=[t_sq], sig=True)
            nw_toks.append(t_nw)
            t_ssq_last = t_mmq
        bk6.wrote(t_ssq_last)
        t_lnr = S.ac(lambda e: e.activation(a_r1[0:1, 0:TA], bk6.ap[0:1, 0:512],
                                            AF.Ln, scale=1.0 / D, bias=c_cval[0:1, 3:4]),
                     deps=bk6.rdeps(), sig=True)
        bk6.read(t_lnr)
        t_rr = S.ac(lambda e: e.activation(a_r1[0:1, TA:2 * TA], a_r1[0:1, 0:TA],
                                           AF.Exp, scale=-0.5), deps=[t_lnr], sig=True)
        bk7 = PB[7]
        t_rc = None
        for k4 in range(4):
            t_rc = mm(bk7.ap[:, k4:k4 + 1],
                      a_r1[0:1, TA + k4 * 128:TA + (k4 + 1) * 128],
                      c_ones_rf[0:1, 0:1], True, True,
                      deps=[t_rr] + (bk7.wdeps() if k4 == 0 else []), sig=(k4 == 3))
        bk7.wrote(t_rc)
        t_rce = S.dv(
            (lambda at_: lambda e: e.tensor_copy(
                st_rstd2[:, at_ * 4:(at_ + 1) * 4], bk7.ap[:, 0:4]))(at),
            deps=bk7.rdeps(), sig=True)
        bk7.read(t_rce)

        s_mm_toks = []
        for fc in range(KC):
            sbi, t_w = load_wslab(3, fc, at)
            bk = PB[fc % 4]
            t_last = None
            for kc in range(KC):
                t_last = mm(bk.ap[:, 0:512],
                            a_wsl[sbi][:, kc * 128:(kc + 1) * 128],
                            a_xno[:, kc * TA:(kc + 1) * TA],
                            kc == 0, kc == KC - 1,
                            deps=([t_w] + nw_toks + (bk.wdeps() if kc == 0 else [])),
                            sig=(kc == KC - 1))
            bk.wrote(t_last)
            wsl_release(sbi, [t_last])
            s_mm_toks.append(t_last)
            sgi = fc % 2
            t_se = S.ac(
                (lambda bkap, s_: lambda e: e.copy(a_ssg[s_][:, :], bkap[:, 0:512]))(
                    bk.ap, sgi),
                deps=bk.rdeps() + ssg_free[sgi], sig=True)
            bk.read(t_se)
            t_sd = S.qa(
                (lambda s_, f_, gg: lambda e: e.dma_start(
                    out=s_dram[f_ * 128:(f_ + 1) * 128, gg:gg + TA], in_=a_ssg[s_][:, :]))(
                    sgi, fc, g0),
                deps=[t_se], ch=f"ssg{sgi}")
            ssg_free[sgi] = [t_sd]

        q_free = [t_q1]
        k_free = [("pe", S.cnt["pe"])]
        v_free = [("pe", S.cnt["pe"]), t_vd]
        ld_free = new_ld_free + [t_cfm]
        xno_free = s_mm_toks[-1:] + nw_toks[-1:]
        if at == 0 and DEV_WPASS:
            emit_wpass()

    if DEV_PREP:
        while PS["gi"] < len(G_UNITS):
            prep_gamma_step()
        if PS["inv"] is None:
            prep_gamma_finalize()
        while PS["qi"] < len(Q_UNITS):
            prep_quant_step()

    FB = S.fence()
    bxn_free = []
    bhid_free = []
    bq1_free = [FB, FB]
    bq2_free = [FB, FB]
    bq3_free = [FB, FB]
    bsig_free = [[], []]
    bg_free = [[], []]
    bs_free = [FB, FB]
    bsg_free = [[], []]
    bx_free = [FB]
    bxt_free = [FB]
    bxc_free = [FB, FB]
    bos_free = [[], []]
    OUT_TOKS = []

    for bt in range(DEV_NBT):
        b = bt // (NBT // BPC)
        t0 = (bt % (NBT // BPC)) * TB
        g0 = bt * TB
        gtt0 = g0 // 128

        xn_evs = []
        for tt in range(TB // 128):
            bi = 0
            ttg = gtt0 + tt
            t_x = S.qs(
                (lambda b_, bb, r0: lambda e: e.dma_start(
                    out=b_x[b_][:, :], in_=x_hbm[bb, r0:r0 + 128, :]))(bi, b, t0 + tt * 128),
                deps=[bx_free[bi]] + FB, ch="bx")
            t_ap = S.ac(
                (lambda b_, c: lambda e: e.activation(
                    b_xt[b_][:, :], b_x[b_][:, :], AF.Identity,
                    scale=st_scale[:, c:c + 1], bias=st_bias[:, c:c + 1]))(bi, ttg),
                deps=[t_x, bxt_free[bi]] + FB, sig=True)
            bx_free[bi] = t_ap
            grp = []
            for g in range(4):
                bk = PB[6 + g % 2]
                t_tl = None
                for j in range(4):
                    fcb = g * 4 + j
                    t_tl = mm(bk.ap[:, j * 128:(j + 1) * 128],
                              b_xt[bi][:, fcb * 128:(fcb + 1) * 128],
                              c_i128b[:, :], True, True,
                              deps=[t_ap] + (bk.wdeps() if j == 0 else []) + FB,
                              sig=(j == 3))
                bk.wrote(t_tl)
                t_ev = S.ac(
                    (lambda bkap, g_, tt_: lambda e: e.copy(
                        view(b_xn, 0, 128, (g_ * 4) * TB + tt_ * 128, [[TB, 4], [1, 128]]),
                        bkap[:, :].rearrange("p (a b) -> p a b", a=4)))(bk.ap, g, tt),
                    deps=bk.rdeps() + bxn_free, sig=True)
                bk.read(t_ev)
                grp.append(t_ev)
            bxt_free[bi] = grp
            xn_evs.extend(grp)
        bxn_free = []

        if bt == 0 and "xnb0" in tap_hbm:
            S.qa(lambda e: e.dma_start(out=tap_hbm["xnb0"][:, :], in_=b_xn[:, :]),
                 deps=xn_evs, ch='tap')

        hid_toks = []
        for hc in range(HC):
            sbi = hc % 2
            t_w1 = S.qs(
                (lambda bb, h_: lambda e: e.dma_start(out=bb[:, :], in_=q1t[h_, :, :]))(
                    b_q1s[sbi], hc),
                deps=[bq1_free[sbi]] + FB, ch=f"bq1_{sbi}")
            t_w2 = S.qs(
                (lambda bb, h_: lambda e: e.dma_start(out=bb[:, :], in_=q2t[h_, :, :]))(
                    b_q2s[sbi], hc),
                deps=[bq2_free[sbi]] + FB, ch=f"bq2_{sbi}")
            pbase1 = 0 if hc % 2 == 0 else 4
            pbase2 = 2 if hc % 2 == 0 else 6
            sig_toks = []
            for half in range(2):
                bk1 = PB[pbase1 + half]
                t_l1 = None
                for kc in range(KC):
                    t_l1 = mm(bk1.ap[:, 0:512],
                              b_q1s[sbi][:, kc * 128:(kc + 1) * 128],
                              b_xn[:, kc * TB + half * 512: kc * TB + (half + 1) * 512],
                              kc == 0, kc == KC - 1,
                              deps=([t_w1] + xn_evs + (bk1.wdeps() if kc == 0 else [])),
                              sig=(kc == KC - 1))
                bk1.wrote(t_l1)
                t_sg = S.ac(
                    (lambda bkap, s_, h_: lambda e: e.activation(
                        b_sig[s_][:, h_ * 512:(h_ + 1) * 512], bkap[:, 0:512],
                        AF.Sigmoid))(bk1.ap, sbi, half),
                    deps=bk1.rdeps() + bsig_free[sbi], sig=True)
                bk1.read(t_sg)
                sig_toks.append(t_sg)
            bsig_free[sbi] = []
            t_w1_last = t_l1
            for half in range(2):
                bk2 = PB[pbase2 + half]
                t_l2 = None
                for kc in range(KC):
                    t_l2 = mm(bk2.ap[:, 0:512],
                              b_q2s[sbi][:, kc * 128:(kc + 1) * 128],
                              b_xn[:, kc * TB + half * 512: kc * TB + (half + 1) * 512],
                              kc == 0, kc == KC - 1,
                              deps=([t_w2] + xn_evs + (bk2.wdeps() if kc == 0 else [])),
                              sig=(kc == KC - 1))
                bk2.wrote(t_l2)
                t_h = S.dv(
                    (lambda bkap, s_, h_, hc_: lambda e: e.tensor_mul(
                        b_hid[:, hc_ * TB + h_ * 512: hc_ * TB + (h_ + 1) * 512],
                        b_sig[s_][:, h_ * 512:(h_ + 1) * 512],
                        bkap[:, 0:512]))(bk2.ap, sbi, half, hc),
                    deps=bk2.rdeps() + [sig_toks[half]] + bhid_free, sig=True)
                bk2.read(t_h)
                hid_toks.append(t_h)
                bsig_free[sbi].append(t_h)
            bq1_free[sbi] = t_w1_last
            bq2_free[sbi] = t_l2
        bxn_free = [t_l2]

        if bt == 0 and "hid0" in tap_hbm:
            S.qa(lambda e: e.dma_start(out=tap_hbm["hid0"][:, :], in_=b_hid[:, :]),
                 deps=hid_toks, ch='tap')

        last_x3 = None
        for fc in range(KC):
            sbi = fc % 2
            t_w3 = S.qs(
                (lambda bb, f_: lambda e: e.dma_start(out=bb[:, :], in_=q3t[f_, :, :]))(
                    b_q3s[sbi], fc),
                deps=[bq3_free[sbi]] + FB, ch=f"bq3_{sbi}")
            t_sl = S.qs(
                (lambda bb, f_, gg: lambda e: e.dma_start(
                    out=bb[:, :], in_=s_dram[f_ * 128:(f_ + 1) * 128, gg:gg + TB]))(
                    b_s[sbi], fc, g0),
                deps=[bs_free[sbi]] + FB, ch=f"bs{sbi}")
            t_xc = S.qs(
                (lambda bb, bb_, r0, f_: lambda e: e.dma_start(
                    out=bb[:, :].rearrange("p (tb f) -> p tb f", tb=8),
                    in_=x_hbm[bb_, r0:r0 + TB, f_ * 128:(f_ + 1) * 128].rearrange(
                        "(tb p) f -> p tb f", p=128)))(b_xc[sbi], b, t0, fc),
                deps=[bxc_free[sbi]] + FB, ch=f"bxc{sbi}")
            pb0 = 0 if fc % 2 == 0 else 2
            g_toks = []
            for half in range(2):
                bk = PB[pb0 + half]
                t_l3 = None
                for hc in range(HC):
                    t_l3 = mm(bk.ap[:, 0:512],
                              b_q3s[sbi][:, hc * 128:(hc + 1) * 128],
                              b_hid[:, hc * TB + half * 512: hc * TB + (half + 1) * 512],
                              hc == 0, hc == HC - 1,
                              deps=([t_w3] + hid_toks + (bk.wdeps() if hc == 0 else [])),
                              sig=(hc == HC - 1))
                bk.wrote(t_l3)
                last_x3 = t_l3
                t_g = S.ac(
                    (lambda bkap, s_, h_: lambda e: e.activation(
                        b_g[s_][:, h_ * 512:(h_ + 1) * 512], bkap[:, 0:512],
                        AF.Silu))(bk.ap, sbi, half),
                    deps=bk.rdeps() + bg_free[sbi], sig=True)
                bk.read(t_g)
                g_toks.append(t_g)
            bq3_free[sbi] = last_x3
            t_sgm = S.dv(
                (lambda s_: lambda e: e.tensor_mul(
                    b_sg[s_][:, :], b_g[s_][:, :], b_s[s_][:, :]))(sbi),
                deps=g_toks + [t_sl] + bsg_free[sbi], sig=True)
            bg_free[sbi] = [t_sgm]
            bs_free[sbi] = t_sgm
            st_toks = []
            for tb in range(8):
                bk = PB[4 + tb % 2]
                t_tp = mm(bk.ap[:, 0:128],
                          b_sg[sbi][:, tb * 128:(tb + 1) * 128],
                          c_i128b[:, :], True, True,
                          deps=[t_sgm] + bk.wdeps(), sig=True)
                bk.wrote(t_tp)
                t_st = S.dv(
                    (lambda bkap, s_, tb_, c: lambda e: e.scalar_tensor_tensor(
                        b_os[s_][:, tb_ * 128:(tb_ + 1) * 128],
                        bkap[:, 0:128], st_rstd2[:, c:c + 1],
                        b_xc[s_][:, tb_ * 128:(tb_ + 1) * 128],
                        ALU.mult, ALU.add))(bk.ap, sbi, tb, gtt0 + tb),
                    deps=bk.rdeps() + [t_xc] + bos_free[sbi], sig=True)
                bk.read(t_st)
                st_toks.append(t_st)
            bos_free[sbi] = []
            bsg_free[sbi] = [("pe", S.cnt["pe"])]
            bxc_free[sbi] = st_toks[-1]
            t_od = S.qa(
                (lambda s_, bb_, r0, f_: lambda e: e.dma_start(
                    out=out_hbm[bb_, r0:r0 + TB, f_ * 128:(f_ + 1) * 128].rearrange(
                        "(tb p) f -> p tb f", p=128),
                    in_=b_os[s_][:, :].rearrange("p (tb f) -> p tb f", tb=8)))(
                    sbi, b, t0, fc),
                deps=st_toks, ch=f"out{sbi}")
            bos_free[sbi] = [t_od]
            OUT_TOKS.append(t_od)
        bhid_free = [last_x3]

    fin = [(ch, S.cnt[ch]) for ch in S.cnt if ch.startswith("out")]
    S.emit("qg", lambda e: e.nop(), deps=fin)

    with nc.Block() as block:
        S.replay(block)

    return nc


def _get_nc():
    if "nc" not in _CACHE:
        _CACHE["nc"] = build_program()
    return _CACHE["nc"]


_CACHE = {}


def kernel(**inputs):
    nc = _get_nc()
    lay = _host_layouts(inputs)
    x = np.asarray(inputs["L0_attn_x"], np.float32)
    in_maps = []
    for k in range(NCORE):
        m = {"x": np.ascontiguousarray(x[BPC * k:BPC * (k + 1)])}
        m.update(lay)
        in_maps.append(m)
    res = run_bass_kernel_spmd(nc, in_maps, core_ids=list(range(NCORE)))
    out = np.concatenate([r["out"] for r in res.results], axis=0)
    return out.astype(np.float32)
